# version 07
# baseline (speedup 1.0000x reference)
import numpy as np
import ml_dtypes
import contextlib
import math
import concourse.bass as bass
import concourse.mybir as mybir
from concourse.bass_utils import run_bass_kernel_spmd

F32 = mybir.dt.float32
BF16 = mybir.dt.bfloat16
I16 = mybir.dt.int16
AF = mybir.ActivationFunctionType
ALU = mybir.AluOpType
AXX = mybir.AxisListType.X

NSEQ, T, D, H, DHD, DR, KVR, QR, FH = 2, 2048, 1024, 16, 64, 32, 256, 512, 4096
NTB = T // 512
ISQ96 = 1.0 / math.sqrt(96.0)
FTILE = 1024
NFT = T // FTILE
LN127 = float(np.log(127.0))


class Sem:
    def __init__(self, stk, nc, name):
        self.h = stk.enter_context(nc.semaphore(name))
        self.n = 0


def build_program(debug=0):
    nc = bass.Bass("TRN2", target_bir_lowering=False, detect_race_conditions=False)

    xin = nc.dram_tensor("xin", [NSEQ, D, T], F32, kind="ExternalInput").ap()
    wkv_d = nc.dram_tensor("wkv", [D, KVR], F32, kind="ExternalInput").ap()
    gkv_d = nc.dram_tensor("gkv", [KVR, 1], F32, kind="ExternalInput").ap()
    wq_d = nc.dram_tensor("wq", [D, QR], F32, kind="ExternalInput").ap()
    wkrope_d = nc.dram_tensor("wkrope", [D, QR], F32, kind="ExternalInput").ap()
    wqrope_d = nc.dram_tensor("wqrope", [QR, QR], F32, kind="ExternalInput").ap()
    wkdec_d = nc.dram_tensor("wkdec", [KVR, D], F32, kind="ExternalInput").ap()
    wvdec_d = nc.dram_tensor("wvdec", [KVR, D], F32, kind="ExternalInput").ap()
    wqdec_d = nc.dram_tensor("wqdec", [QR, D], F32, kind="ExternalInput").ap()
    wo_d = nc.dram_tensor("wo", [D, D], F32, kind="ExternalInput").ap()
    w1t_d = nc.dram_tensor("w1t", [128, 32768], F32, kind="ExternalInput").ap()
    w2t_d = nc.dram_tensor("w2t", [1024, 4096], F32, kind="ExternalInput").ap()
    convw_d = nc.dram_tensor("convw", [FH, 3], F32, kind="ExternalInput").ap()
    convb_d = nc.dram_tensor("convb", [FH, 1], F32, kind="ExternalInput").ap()
    cos_d = nc.dram_tensor("cosTc", [128, T], BF16, kind="ExternalInput").ap()
    sin_d = nc.dram_tensor("sinTc", [128, T], BF16, kind="ExternalInput").ap()
    ident_d = nc.dram_tensor("identc", [128, 128], BF16, kind="ExternalInput").ap()
    identf_d = nc.dram_tensor("identf", [128, 128], F32, kind="ExternalInput").ap()
    onesr_d = nc.dram_tensor("onesr", [1, 128], BF16, kind="ExternalInput").ap()
    e2_d = nc.dram_tensor("e2c", [2, 128], BF16, kind="ExternalInput").ap()
    atri_d = nc.dram_tensor("atri", [128, 128], BF16, kind="ExternalInput").ap()
    btri_d = nc.dram_tensor("btri", [128, 128], BF16, kind="ExternalInput").ap()
    ocole_d = nc.dram_tensor("ocole", [128, 1], BF16, kind="ExternalInput").ap()
    ocolo_d = nc.dram_tensor("ocolo", [128, 1], BF16, kind="ExternalInput").ap()
    x1d = nc.dram_tensor("x1d", [NSEQ, D, T], F32, kind="Internal").ap()
    w1q_s = nc.dram_tensor("w1q_s", [128, 32768], BF16, kind="Internal").ap()
    w2q_s = nc.dram_tensor("w2q_s", [1024, 4096], BF16, kind="Internal").ap()
    outd = nc.dram_tensor("outd", [NSEQ, D, T], F32, kind="ExternalOutput").ap()

    dbg = {}

    def dbg_t(name, shape, dt=F32):
        if debug:
            dbg[name] = nc.dram_tensor("dbg_" + name, shape, dt, kind="ExternalOutput").ap()

    dbg_t("x_ln", [NSEQ, D, T], BF16)
    dbg_t("kvn", [NSEQ, KVR, T], BF16)
    dbg_t("qc", [NSEQ, QR, T], BF16)
    dbg_t("ropek", [NSEQ, 2, 128, 2 * T], BF16)
    dbg_t("ropeq", [NSEQ, 2, 128, 2 * T], BF16)
    dbg_t("kring", [4, 128, T], BF16)
    dbg_t("qring", [4, 128, T], BF16)
    dbg_t("vring", [2, 128, 16 * 128], BF16)
    dbg_t("outT", [NSEQ, D, T], BF16)
    dbg_t("lrows", [NSEQ, 16, T], BF16)
    dbg_t("x1", [NSEQ, D, T])
    dbg_t("wrows", [3, 8])
    dbg_t("w1q", [128, 32768], BF16)
    dbg_t("w2q", [1024, 4096], BF16)
    dbg_t("xq", [1, D, 1026], BF16)
    dbg_t("hq", [1, FH, 1024], BF16)
    dbg_t("hsil", [1, FH, 1024], BF16)
    dbg_t("mxh", [128, 96])
    dbg_t("sbch", [128, 1024], BF16)
    dbg_t("treeD", [128, 1026], BF16)
    dbg_t("stgd2", [128, 1026], BF16)
    dbg_t("psdump", [128, 128])

    stk = contextlib.ExitStack()
    with stk:
        E = stk.enter_context

        UP = E(nc.sbuf_tensor("UP", [128, 57344], BF16))
        OFF_XLN, OFF_OUTT, OFF_KVN, OFF_QC = 0, 0, 16384, 20480
        OFF_ROPE = 28672
        OFF_KR, OFF_QRG, OFF_VR = 45056, 49152, 53248
        OFF_HBUF, OFF_X1B, OFF_X2LN = 0, 32832, 41040
        OFF_XQ = OFF_X2LN

        def up(p0, pn, off, cols):
            return UP[p0:p0 + pn, off: off + cols]

        WP1 = E(nc.sbuf_tensor("WP1", [128, 10240], BF16))
        WP2 = E(nc.sbuf_tensor("WP2", [128, 10240], BF16))
        cosT = E(nc.sbuf_tensor("cosT", [128, T], BF16))
        sinT = E(nc.sbuf_tensor("sinT", [128, T], BF16))
        identb = E(nc.sbuf_tensor("identb", [128, 128], BF16))
        onesrow = E(nc.sbuf_tensor("onesrow", [1, 128], BF16))
        onescol = E(nc.sbuf_tensor("onescol", [128, 1], BF16))
        ocol_e = E(nc.sbuf_tensor("ocol_e", [128, 1], BF16))
        ocol_o = E(nc.sbuf_tensor("ocol_o", [128, 1], BF16))
        e2sel = E(nc.sbuf_tensor("e2sel", [2, 128], BF16))
        atri = E(nc.sbuf_tensor("atri_s", [128, 128], BF16))
        btri = E(nc.sbuf_tensor("btri_s", [128, 128], BF16))
        gkv = E(nc.sbuf_tensor("gkvc", [128, 2], F32))
        convw = E(nc.sbuf_tensor("convwc", [128, 96], F32))
        convb = E(nc.sbuf_tensor("convbc", [128, 32], F32))
        Lbuf = E(nc.sbuf_tensor("Lbuf", [16, T], F32))
        LbufR = E(nc.sbuf_tensor("LbufR", [16, T], BF16))
        ksq = E(nc.sbuf_tensor("ksq", [128, T], BF16))
        rl2 = ksq[0:2, :]
        rows = E(nc.sbuf_tensor("rows", [128, 1024], F32))
        rwsB = E(nc.sbuf_tensor("rwsB", [128, 1024], BF16))
        onesrow_f = E(nc.sbuf_tensor("onesrow_f", [128, 128], F32))
        cc = E(nc.sbuf_tensor("cc", [128, 8], F32))
        gcol = E(nc.sbuf_tensor("gcol", [128, 8], F32))
        mxcol = E(nc.sbuf_tensor("mxcol", [128, 96], F32))
        identf = E(nc.sbuf_tensor("identf_s", [128, 128], F32))
        stgA = E(nc.sbuf_tensor("stgA", [128, 512], F32))
        stgB = E(nc.sbuf_tensor("stgB", [128, 512], F32))
        stgC = E(nc.sbuf_tensor("stgC", [128, 1026], BF16))
        stgD = E(nc.sbuf_tensor("stgD", [128, 1026], BF16))
        stgI = E(nc.sbuf_tensor("stgI", [128, 1026], I16))
        attnT = E(nc.sbuf_tensor("attnT", [128, 3 * 512], BF16))
        projs = E(nc.sbuf_tensor("projs", [128, 4096], BF16))

        def wqbuf(b):
            return UP[0:128, 2048 * b:2048 * (b + 1)].bitcast(F32)

        xbstg = UP[0:128, OFF_ROPE: OFF_ROPE + 4096]
        ffout = UP[0:128, 49248: 49248 + 4096]
        outstg = UP[0:128, 53344: 53344 + 1024].bitcast(F32)

        psA = E(nc.psum_tensor("psA", [128, 1536], F32))
        psB = E(nc.psum_tensor("psB", [128, 2048], F32))
        psC = E(nc.psum_tensor("psC", [128, 512], F32))

        sW = Sem(stk, nc, "sW")
        sG = Sem(stk, nc, "sG")
        sT = Sem(stk, nc, "sT")
        sV = Sem(stk, nc, "sV")
        sS = Sem(stk, nc, "sS")
        sO = Sem(stk, nc, "sO")

        Q = {k: [] for k in ("sync", "gpsimd", "tensor", "vector", "scalar")}
        REG = {}

        def em(eng, fn, waits=(), inc=None, inck=1):
            if inc is not None:
                inc.n += inck
            wl = []
            seen = {}
            for (s_, v_) in waits:
                if v_ > 0 and v_ > seen.get(id(s_), -1):
                    seen[id(s_)] = max(seen.get(id(s_), -1), v_)
            best = {}
            for (s_, v_) in waits:
                if v_ > 0:
                    k_ = id(s_.h)
                    if k_ not in best or v_ > best[k_][1]:
                        best[k_] = (s_.h, v_)
            wl = list(best.values())
            ih = inc.h if inc is not None else None

            def run(e, wl=wl, fn=fn, ih=ih, inck=inck):
                for h, v in wl:
                    e.wait_ge(h, v)
                ins = fn(e)
                if ih is not None:
                    if ins is None:
                        ins = e.nop()
                    ins.then_inc(ih, inck)
            Q[eng].append(run)
            return inc.n if inc is not None else None

        def consume(region, sem):
            REG[region] = [(sem, sem.n)]

        def consume_add(region, sem):
            lst = REG.setdefault(region, [])
            lst[:] = [(s_, v_) for (s_, v_) in lst if s_ is not sem]
            lst.append((sem, sem.n))

        def rwait(*regions):
            out = []
            for r in regions:
                out.extend(REG.get(r, []))
            return out

        def dma(eng, out, in_, sem, waits=()):
            return em(eng, lambda e, o=out, i=in_: e.dma_start(out=o, in_=i),
                      waits=waits, inc=sem, inck=16)

        dma("sync", identf[:], identf_d, sW)
        for t_, d_ in ((cosT, cos_d), (sinT, sin_d), (identb, ident_d), (onesrow, onesr_d),
                       (e2sel, e2_d), (atri, atri_d), (btri, btri_d),
                       (ocol_e, ocole_d), (ocol_o, ocolo_d)):
            dma("sync", t_[:], d_, sW)
        dma("sync", gkv[:, 0:1], gkv_d[0:128, :], sW)
        dma("sync", gkv[:, 1:2], gkv_d[128:256, :], sW)
        for ch in range(32):
            dma("sync", convw[:, 3 * ch:3 * ch + 3], convw_d[128 * ch:128 * ch + 128, :], sW)
            dma("sync", convb[:, ch:ch + 1], convb_d[128 * ch:128 * ch + 128, :], sW)
        w_const = sW.n

        for fn in (lambda e: e.memset(onescol[:], 1.0),
                   lambda e: e.memset(cc[:, 0:1], 1e-6),
                   lambda e: e.memset(cc[:, 1:2], 1e-5),
                   lambda e: e.memset(cc[:, 2:3], LN127),
                   lambda e: e.memset(cc[:, 3:4], 0.0),
                   lambda e: e.memset(cc[:, 4:5], 1e-10),
                   lambda e: e.memset(onesrow_f[:], 1.0)):
            em("vector", fn, inc=sV)
        v_preset = sV.n

        def emit_ring_presets(extra_waits):
            em("vector", lambda e: e.memset(up(0, 128, OFF_KR, 2 * T), 0.0),
               waits=extra_waits + rwait("Kc"), inc=sV)
            em("vector", lambda e: e.memset(up(0, 128, OFF_QRG, 2 * T), 0.0),
               waits=rwait("Qc"), inc=sV)
            em("vector", lambda e: e.memset(up(0, 128, OFF_VR, 2 * 2048), 0.0),
               waits=rwait("Vs"), inc=sV)
            em("vector", lambda e: e.memset(up(64, 1, OFF_KR, T), -1.0), inc=sV)
            em("vector", lambda e: e.memset(up(32, 1, OFF_KR + T, T), -1.0), inc=sV)
            em("vector", lambda e: e.memset(
                UP[0:128, OFF_VR: OFF_VR + 2048].rearrange("p (k c) -> p k c", c=128)[:, :, 64:65],
                1.0), inc=sV)
            em("vector", lambda e: e.memset(
                UP[0:128, OFF_VR + 2048: OFF_VR + 4096].rearrange("p (k c) -> p k c",
                                                                 c=128)[:, :, 0:1],
                1.0), inc=sV)
            for r in ("Kc", "Qc", "Vs"):
                consume_add(r, sV)
            return sV.n

        def w_pieces(wd):
            nr, ncol = wd.shape
            for ch in range(nr // 128):
                for sub in range(ncol // 1024):
                    yield wd[128 * ch:128 * (ch + 1), 1024 * sub:1024 * (sub + 1)]

        def w_scale_pass(wd, nch, prow):
            pieces = list(w_pieces(wd))
            total = len(pieces)
            npc = 0
            for piece in pieces:
                    buf = npc % 2
                    first = (npc == 0)
                    last = (npc == total - 1)
                    npc += 1
                    wsl = wqbuf(buf)
                    ld = dma("sync", wsl, piece, sW,
                             waits=rwait(("wq", buf)))
                    ab = projs[:, 1024 * buf:1024 * (buf + 1)]
                    em("vector", lambda e, wsl=wsl, ab=ab:
                       e.scalar_tensor_tensor(ab, wsl, -1.0, wsl, ALU.mult, ALU.max),
                       waits=[(sW, ld)] + rwait(("wabs", buf), "projs"), inc=sV)
                    consume(("wq", buf), sV)
                    vabs = sV.n
                    for blk in range(2):
                        st = (first and blk == 0)
                        sp = (last and blk == 1)
                        em("tensor", lambda e, ab=ab, blk=blk, st=st, sp=sp, prow=prow:
                           e.matmul(psC[prow:prow + 1, 0:512], onescol[:],
                                    ab[:, 512 * blk:512 * (blk + 1)], start=st, stop=sp),
                           waits=([(sV, vabs), (sV, v_preset)] + (rwait("psC") if st else []))
                           if blk == 0 else [],
                           inc=(sT if blk == 1 else None))
                    consume(("wabs", buf), sT)
            return sT.n

        t_w1s = w_scale_pass(w1t_d, 8, 0)
        t_w2s = w_scale_pass(w2t_d, 32, 32)
        em("vector", lambda e: e.reduce_sum(rows[0:1, 0:1], psC[0:1, 0:512], AXX),
           waits=[(sT, t_w1s)], inc=sV)
        em("vector", lambda e: e.reduce_sum(rows[32:33, 0:1], psC[32:33, 0:512], AXX),
           waits=[(sT, t_w2s)], inc=sV)
        consume("psC", sV)
        em("vector", lambda e: e.tensor_scalar(rows[0:1, 1:2], rows[0:1, 0:1],
                                               1.0 / (D * FH), 1e-5, ALU.mult, ALU.max), inc=sV)
        em("vector", lambda e: e.tensor_scalar(rows[32:33, 1:2], rows[32:33, 0:1],
                                               1.0 / (D * FH), 1e-5, ALU.mult, ALU.max), inc=sV)
        v_means = sV.n
        em("scalar", lambda e: e.activation(rows[0:1, 2:3], rows[0:1, 1:2], AF.Ln),
           waits=[(sV, v_means)], inc=sS)
        em("scalar", lambda e: e.activation(rows[32:33, 2:3], rows[32:33, 1:2], AF.Ln), inc=sS)
        em("scalar", lambda e: e.activation(rows[0:1, 3:4], rows[0:1, 2:3], AF.Exp, scale=-1.0), inc=sS)
        em("scalar", lambda e: e.activation(rows[32:33, 3:4], rows[32:33, 2:3], AF.Exp, scale=-1.0), inc=sS)
        s_winv = sS.n
        em("vector", lambda e: e.tensor_scalar(rows[0:1, 4:5], rows[0:1, 2:3], -LN127, None, ALU.add),
           waits=[(sS, s_winv)], inc=sV)
        v_deqb = sV.n
        specs = ((0, 3, 1, [(sS, s_winv)]), (32, 3, 2, [(sS, s_winv)]),
                 (0, 4, 3, [(sV, v_deqb)]))
        for k, (sp, sc_, dc, wv) in enumerate(specs):
            for half in (0, 64):
                for rep in range(2 if (k == 2 and half == 64) else 1):
                    em("tensor", lambda e, sp=sp, sc_=sc_, k=k, half=half:
                       e.matmul(psC[half:half + 64, k:k + 1], onesrow_f[sp:sp + 1, 0:64],
                                rows[sp:sp + 1, sc_:sc_ + 1], start=True, stop=True),
                       waits=(wv + rwait("psC") + [(sV, v_preset)])
                       if (k == 0 and half == 0) else (wv if half == 0 else []),
                       inc=sT)
        t_b = sT.n
        for rnd in range(2):
            for k, (sp, sc_, dc, wv) in enumerate(specs):
                em("vector", lambda e, dc=dc, k=k:
                   e.tensor_copy(gcol[:, dc:dc + 1], psC[:, k:k + 1]),
                   waits=[(sT, t_b)] if (k == 0 and rnd == 0) else [], inc=sV)
        consume("psC", sV)
        v_wcols = sV.n

        def w_quant_pass(wd, wq_out, nch, swcol):
            npc = 0
            for piece, opiece in zip(w_pieces(wd), w_pieces(wq_out)):
                    buf = npc % 2
                    npc += 1
                    wsl = wqbuf(buf)
                    ld = dma("sync", wsl, piece, sW,
                             waits=rwait(("wq", buf)))
                    cbuf = wqbuf(2)
                    em("vector", lambda e, wsl=wsl, swcol=swcol, cbuf=cbuf:
                       e.tensor_scalar(cbuf, wsl, swcol, 1.25, ALU.mult, ALU.min),
                       waits=[(sW, ld), (sV, v_wcols)] + rwait("stgIq", "stgCD", ("wq", 2)), inc=sV)
                    em("vector", lambda e, cbuf=cbuf:
                       e.tensor_scalar(cbuf, cbuf, -1.25, None, ALU.max),
                       inc=sV)
                    consume(("wq", buf), sV)
                    em("vector", lambda e, cbuf=cbuf:
                       e.tensor_scalar(stgI[:, 0:1024], cbuf, 256.5, None, ALU.add),
                       inc=sV)
                    em("vector", lambda e, cbuf=cbuf:
                       e.scalar_tensor_tensor(stgD[:, 0:1024], stgI[:, 0:1024], -256.5,
                                              cbuf, ALU.add, ALU.is_gt), inc=sV)
                    consume("stgIq", sV)
                    consume(("wq", 2), sV)
                    em("vector", lambda e:
                       e.scalar_tensor_tensor(stgD[:, 0:1024], stgI[:, 0:1024], -256.0,
                                              stgD[:, 0:1024], ALU.add, ALU.subtract), inc=sV)
                    vq = sV.n
                    dma("gpsimd", opiece, stgD[:, 0:1024], sG, waits=[(sV, vq)])
                    consume("stgCD", sG)

        if debug:
            dma("sync", dbg["wrows"][0], rows[0:1, 0:8], sO, waits=[(sV, v_wcols)])
            dma("sync", dbg["wrows"][1], rows[32:33, 0:8], sO, waits=[(sV, v_wcols)])
            dma("sync", dbg["wrows"][2], gcol[0:1, 0:8], sO, waits=[(sV, v_wcols)])
        w_quant_pass(w1t_d, w1q_s, 8, gcol[:, 1:2])
        w_quant_pass(w2t_d, w2q_s, 32, gcol[:, 2:3])
        g_wq_done = sG.n
        if debug:
            dma("sync", dbg["w1q"][:, :], w1q_s[:, :], sO, waits=[(sG, g_wq_done)])
            for ch in range(8):
                dma("sync", dbg["w2q"][128 * ch:128 * (ch + 1), :],
                    w2q_s[128 * ch:128 * (ch + 1), :], sO, waits=[(sG, g_wq_done)])

        def xln_ap(ch, c0, n):
            return up(0, 128, OFF_XLN + ch * T + c0, n)

        def kvn_ap(rch, c0, n):
            return up(0, 128, OFF_KVN + rch * T + c0, n)

        def qc_ap(qch, c0, n):
            return up(0, 128, OFF_QC + qch * T + c0, n)

        def rope_ap(idx, p0, pn, c0, n):
            return up(p0, pn, OFF_ROPE + idx * 4096 + c0, n)

        def kslot(sl, c0, n, p0=0, pn=128):
            return up(p0, pn, OFF_KR + sl * T + c0, n)

        def qslot(sl, c0, n, p0=0, pn=128):
            return up(p0, pn, OFF_QRG + sl * T + c0, n)

        def vslot(sl, c0, n, p0=0, pn=128):
            return up(p0, pn, OFF_VR + sl * 2048 + c0, n)

        def outT_ap(ch, c0, n, p0=0, pn=128):
            return up(p0, pn, OFF_OUTT + ch * T + c0, n)

        def emit_ln_rows(t_sum, t_ssq, bn=512):
            em("vector", lambda e, bn=bn: e.tensor_copy(rows[0:1, 512:512 + bn], psC[0:1, 0:bn]),
               waits=[(sT, t_sum), (sT, t_ssq)], inc=sV)
            em("vector", lambda e, bn=bn: e.tensor_tensor(rows[0:1, 0:bn], rows[0:1, 512:512 + bn],
                                                          rows[0:1, 512:512 + bn], ALU.mult),
               inc=sV)
            em("vector", lambda e, bn=bn:
               e.scalar_tensor_tensor(rows[0:1, 0:bn], rows[0:1, 0:bn], -1.0 / D,
                                      psC[32:33, 0:bn], ALU.mult, ALU.add),
               waits=[(sT, t_ssq)], inc=sV)
            consume("psC", sV)
            em("vector", lambda e, bn=bn:
               e.tensor_scalar(rows[0:1, 0:bn], rows[0:1, 0:bn], 1.0 / D, 1e-5,
                               ALU.mult, ALU.add), inc=sV)
            v_veps = sV.n
            em("scalar", lambda e, bn=bn: e.activation(rows[0:1, 0:bn], rows[0:1, 0:bn], AF.Ln),
               waits=[(sV, v_veps)], inc=sS)
            em("scalar", lambda e, bn=bn: e.activation(rwsB[0:1, 0:bn], rows[0:1, 0:bn],
                                                       AF.Exp, scale=-0.5), inc=sS)
            s_rstd = sS.n
            em("vector", lambda e, bn=bn:
               e.scalar_tensor_tensor(rwsB[0:1, 512:512 + bn], rows[0:1, 512:512 + bn], -1.0 / D,
                                      rwsB[0:1, 0:bn], ALU.mult, ALU.mult),
               waits=[(sS, s_rstd)], inc=sV)
            return s_rstd, sV.n

        def emit_AB_bcast(s_rstd, v_Brow, bn=512):
            em("tensor", lambda e, bn=bn: e.matmul(psA[:, 0:bn], onesrow[:], rwsB[0:1, 0:bn],
                                                   start=True, stop=True),
               waits=[(sS, s_rstd)] + rwait(("psA", 0)), inc=sT)
            em("tensor", lambda e, bn=bn: e.matmul(psA[:, 512:512 + bn], onesrow[:],
                                                   rwsB[0:1, 512:512 + bn], start=True, stop=True),
               waits=[(sV, v_Brow)] + rwait(("psA", 1)), inc=sT)
            return sT.n

        for s in range(NSEQ):
            wkv_sb = WP1[:, 0:2048]
            wq_sb = WP1[:, 2048:6144]
            wkrope_sb = WP1[:, 6144:10240]
            for ch in range(8):
                dma("gpsimd", wkv_sb[:, 256 * ch:256 * (ch + 1)],
                    wkv_d[128 * ch:128 * (ch + 1), :], sG, waits=rwait("WP1"))
            for ch in range(8):
                dma("gpsimd", wq_sb[:, 512 * ch:512 * (ch + 1)],
                    wq_d[128 * ch:128 * (ch + 1), :], sG)
            for ch in range(8):
                dma("gpsimd", wkrope_sb[:, 512 * ch:512 * (ch + 1)],
                    wkrope_d[128 * ch:128 * (ch + 1), :], sG)
            g_p1 = sG.n
            if s == 0:
                wkdec_sb = WP2[:, 0:2048]
                wvdec_sb = WP2[:, 2048:4096]
                wqdec_sb = WP2[:, 4096:8192]
                wqrope_sb = WP2[:, 8192:10240]
                for ch in range(2):
                    dma("gpsimd", wkdec_sb[:, 1024 * ch:1024 * (ch + 1)],
                        wkdec_d[128 * ch:128 * (ch + 1), :], sG)
                    dma("gpsimd", wvdec_sb[:, 1024 * ch:1024 * (ch + 1)],
                        wvdec_d[128 * ch:128 * (ch + 1), :], sG)
                for ch in range(4):
                    dma("gpsimd", wqdec_sb[:, 1024 * ch:1024 * (ch + 1)],
                        wqdec_d[128 * ch:128 * (ch + 1), :], sG)
                for ch in range(4):
                    dma("gpsimd", wqrope_sb[:, 512 * ch:512 * (ch + 1)],
                        wqrope_d[128 * ch:128 * (ch + 1), :], sG)
            g_p2 = sG.n

            for tb in range(NTB):
                ap_in = xin[s].rearrange("(c p) t -> p c t", p=128)[:, :, 512 * tb:512 * (tb + 1)]
                g_x = dma("gpsimd", xbstg[:].rearrange("p (c t) -> p c t", c=8), ap_in, sG,
                          waits=rwait("xbstg"))
                for ch in range(8):
                    em("vector", lambda e, ch=ch:
                       e.tensor_tensor(projs[:, 512 * ch:512 * (ch + 1)],
                                       xbstg[:, 512 * ch:512 * (ch + 1)],
                                       xbstg[:, 512 * ch:512 * (ch + 1)], ALU.mult),
                       waits=([(sG, g_x)] + rwait("projs")) if ch == 0 else [], inc=sV)
                v_sq = sV.n
                for ch in range(8):
                    em("tensor", lambda e, ch=ch:
                       e.matmul(psC[0:1, 0:512], onescol[:], xbstg[:, 512 * ch:512 * (ch + 1)],
                                start=(ch == 0), stop=(ch == 7)),
                       waits=([(sG, g_x), (sV, v_preset)] + rwait("psC")) if ch == 0 else [],
                       inc=(sT if ch == 7 else None))
                t_sum = sT.n
                for ch in range(8):
                    em("tensor", lambda e, ch=ch:
                       e.matmul(psC[32:33, 0:512], onescol[:], projs[:, 512 * ch:512 * (ch + 1)],
                                start=(ch == 0), stop=(ch == 7)),
                       waits=[(sV, v_sq)] if ch == 0 else [],
                       inc=(sT if ch == 7 else None))
                t_ssq = sT.n
                consume_add("projs", sT)
                s_rstd, v_Brow = emit_ln_rows(t_sum, t_ssq)
                t_bc = emit_AB_bcast(s_rstd, v_Brow)
                for ch in range(8):
                    em("vector", lambda e, ch=ch:
                       e.tensor_tensor(stgA[:, 0:512], xbstg[:, 512 * ch:512 * (ch + 1)],
                                       psA[:, 0:512], ALU.mult),
                       waits=([(sT, t_bc)] + (rwait("xlnreg", ("wq", 0), ("wq", 1), ("wq", 2))
                                              if (tb == 0 and ch == 0) else []))
                       if ch == 0 else [], inc=sV)
                    em("vector", lambda e, ch=ch, tb=tb:
                       e.tensor_tensor(xln_ap(ch, 512 * tb, 512), stgA[:, 0:512],
                                       psA[:, 512:1024], ALU.add), inc=sV)
                consume("xbstg", sV)
                consume_add("xbstg", sT)
                consume_add("projs", sV)
                consume(("psA", 0), sV)
                consume(("psA", 1), sV)
            v_xln = sV.n
            if debug:
                for ch in range(8):
                    dma("sync", dbg["x_ln"][s][128 * ch:128 * (ch + 1), :], xln_ap(ch, 0, T),
                        sO, waits=[(sV, v_xln)])

            for tb in range(NTB):
                c0 = 512 * tb
                for rch in range(2):
                    for ch in range(8):
                        em("tensor", lambda e, rch=rch, ch=ch, c0=c0:
                           e.matmul(psB[:, 512 * rch:512 * (rch + 1)],
                                    wkv_sb[:, 256 * ch + 128 * rch: 256 * ch + 128 * (rch + 1)],
                                    xln_ap(ch, c0, 512), start=(ch == 0), stop=(ch == 7)),
                           waits=([(sG, g_p1), (sV, v_xln)] + rwait(("psB", rch))) if ch == 0 else [],
                           inc=(sT if ch == 7 else None))
                t_kv = sT.n
                for rch in range(2):
                    em("vector", lambda e, rch=rch:
                       e.tensor_copy(projs[:, 512 * rch:512 * (rch + 1)],
                                     psB[:, 512 * rch:512 * (rch + 1)]),
                       waits=([(sT, t_kv)] + rwait("ksq", "projs")) if rch == 0 else [], inc=sV)
                    em("vector", lambda e, rch=rch:
                       e.tensor_tensor(ksq[:, 512 * rch:512 * (rch + 1)],
                                       projs[:, 512 * rch:512 * (rch + 1)],
                                       projs[:, 512 * rch:512 * (rch + 1)], ALU.mult), inc=sV)
                v_kvsq = sV.n
                for rch in range(2):
                    em("tensor", lambda e, rch=rch:
                       e.matmul(psC[0:1, 0:512], onescol[:], ksq[:, 512 * rch:512 * (rch + 1)],
                                start=(rch == 0), stop=(rch == 1)),
                       waits=([(sV, v_kvsq)] + rwait("psC")) if rch == 0 else [],
                       inc=(sT if rch == 1 else None))
                t_kvss = sT.n
                consume("ksq", sT)
                em("scalar", lambda e: e.activation(rows[0:1, 0:512], psC[0:1, 0:512], AF.Ln,
                                                    scale=1.0 / KVR, bias=cc[0:1, 0:1]),
                   waits=[(sT, t_kvss)], inc=sS)
                consume("psC", sS)
                em("scalar", lambda e: e.activation(rwsB[0:1, 0:512], rows[0:1, 0:512],
                                                    AF.Exp, scale=-0.5), inc=sS)
                s_rms = sS.n
                em("tensor", lambda e: e.matmul(psA[:, 1024:1536], onesrow[:], rwsB[0:1, 0:512],
                                                start=True, stop=True),
                   waits=[(sS, s_rms)] + rwait(("psA", 2)), inc=sT)
                t_rmsb = sT.n
                for rch in range(2):
                    em("vector", lambda e, rch=rch, c0=c0:
                       e.scalar_tensor_tensor(kvn_ap(rch, c0, 512),
                                              projs[:, 512 * rch:512 * (rch + 1)],
                                              gkv[:, rch:rch + 1], psA[:, 1024:1536],
                                              ALU.mult, ALU.mult),
                       waits=[(sT, t_rmsb)] + (rwait("kvnreg") if (tb == 0 and rch == 0) else []),
                       inc=sV)
                consume(("psB", 0), sV)
                consume(("psB", 1), sV)
                consume(("psA", 2), sV)
                consume_add("projs", sV)
                for qch in range(4):
                    blk = 2 + (qch % 2)
                    for ch in range(8):
                        em("tensor", lambda e, qch=qch, ch=ch, c0=c0, blk=blk:
                           e.matmul(psB[:, 512 * blk:512 * (blk + 1)],
                                    wq_sb[:, 512 * ch + 128 * qch: 512 * ch + 128 * (qch + 1)],
                                    xln_ap(ch, c0, 512), start=(ch == 0), stop=(ch == 7)),
                           waits=(rwait(("psB", blk))) if ch == 0 else [],
                           inc=(sT if ch == 7 else None))
                    t_q = sT.n
                    em("vector", lambda e, qch=qch, c0=c0, blk=blk:
                       e.tensor_scalar(qc_ap(qch, c0, 512), psB[:, 512 * blk:512 * (blk + 1)],
                                       ISQ96, None, ALU.mult),
                       waits=[(sT, t_q)] + (rwait("qcreg") if (tb == 0 and qch == 0) else []),
                       inc=sV)
                    consume(("psB", blk), sV)
                v_qc = sV.n

                def rope_block(widx, wsb, nkc, src_ap_fn, extra_w, c0=c0, tb=tb):
                    for half in range(2):
                        for ab in range(2):
                            kch = 2 * ab + half
                            for ch in range(nkc):
                                em("tensor", lambda e, kch=kch, ch=ch, ab=ab, wsb=wsb,
                                   src_ap_fn=src_ap_fn, nkc=nkc:
                                   e.matmul(psA[:, 512 * ab:512 * (ab + 1)],
                                            wsb[:, 512 * ch + 128 * kch: 512 * ch + 128 * (kch + 1)],
                                            src_ap_fn(ch), start=(ch == 0), stop=(ch == nkc - 1)),
                                   waits=(extra_w + rwait(("psA", ab))) if ch == 0 else [],
                                   inc=(sT if ch == nkc - 1 else None))
                        t_r = sT.n
                        cs = cosT[:, c0:c0 + 512]
                        sn = sinT[:, c0:c0 + 512]
                        em("vector", lambda e, cs=cs: e.tensor_tensor(stgA[:, 0:512], psA[:, 0:512],
                                                                      cs, ALU.mult),
                           waits=[(sT, t_r)] + (rwait("ropereg") if (tb == 0 and half == 0) else []),
                           inc=sV)
                        em("vector", lambda e, sn=sn: e.tensor_tensor(stgB[:, 0:512],
                                                                      psA[:, 512:1024], sn,
                                                                      ALU.mult), inc=sV)
                        em("vector", lambda e, half=half, widx=widx, c0=c0:
                           e.tensor_tensor(rope_ap(widx, 0, 128, half * T + c0, 512),
                                           stgA[:, 0:512], stgB[:, 0:512], ALU.subtract), inc=sV)
                        em("vector", lambda e, cs=cs: e.tensor_tensor(stgA[:, 0:512],
                                                                      psA[:, 512:1024], cs,
                                                                      ALU.mult), inc=sV)
                        em("vector", lambda e, sn=sn: e.tensor_tensor(stgB[:, 0:512], psA[:, 0:512],
                                                                      sn, ALU.mult), inc=sV)
                        em("vector", lambda e, half=half, widx=widx, c0=c0:
                           e.tensor_tensor(rope_ap(widx + 1, 0, 128, half * T + c0, 512),
                                           stgA[:, 0:512], stgB[:, 0:512], ALU.add), inc=sV)
                        consume(("psA", 0), sV)
                        consume(("psA", 1), sV)

                rope_block(0, wkrope_sb, 8, lambda ch, c0=c0: xln_ap(ch, c0, 512), [])
                rope_block(2, wqrope_sb, 4, lambda ch, c0=c0: qc_ap(ch, c0, 512),
                           [(sV, v_qc), (sG, g_p2)])
            v_proj = sV.n
            t_proj = sT.n
            consume_add("WP1", sT)
            if debug:
                for rch in range(2):
                    dma("sync", dbg["kvn"][s][128 * rch:128 * (rch + 1), :], kvn_ap(rch, 0, T),
                        sO, waits=[(sV, v_proj)])
                for qch in range(4):
                    dma("sync", dbg["qc"][s][128 * qch:128 * (qch + 1), :], qc_ap(qch, 0, T),
                        sO, waits=[(sV, v_proj)])
                for i, nm in ((0, "ropek"), (2, "ropeq")):
                    for j2 in range(2):
                        dma("sync", dbg[nm][s][j2], rope_ap(i + j2, 0, 128, 0, 2 * T),
                            sO, waits=[(sV, v_proj)])

            def prep_pair(pair):
                h0 = 2 * pair
                for tb in range(4):
                    c0 = 512 * tb
                    for rch in range(2):
                        em("tensor", lambda e, rch=rch, c0=c0, pair=pair:
                           e.matmul(psA[:, 0:512],
                                    wkdec_sb[:, 1024 * rch + 128 * pair: 1024 * rch + 128 * (pair + 1)],
                                    kvn_ap(rch, c0, 512), start=(rch == 0), stop=(rch == 1)),
                           waits=([(sV, v_proj)] + rwait(("psA", 0))) if rch == 0 else [],
                           inc=(sT if rch == 1 else None))
                    t_k = sT.n
                    em("vector", lambda e, c0=c0: e.tensor_copy(kslot(0, c0, 512, 0, 64),
                                                                psA[0:64, 0:512]),
                       waits=[(sT, t_k)] + rwait("Kc"), inc=sV)
                    em("vector", lambda e, c0=c0: e.tensor_copy(kslot(1, c0, 512, 64, 64),
                                                                psA[64:128, 0:512]), inc=sV)
                    consume(("psA", 0), sV)
                for tb in range(4):
                    c0 = 512 * tb
                    for qch in range(4):
                        em("tensor", lambda e, qch=qch, c0=c0, pair=pair:
                           e.matmul(psA[:, 512:1024],
                                    wqdec_sb[:, 1024 * qch + 128 * pair: 1024 * qch + 128 * (pair + 1)],
                                    qc_ap(qch, c0, 512), start=(qch == 0), stop=(qch == 3)),
                           waits=(rwait(("psA", 1))) if qch == 0 else [],
                           inc=(sT if qch == 3 else None))
                    t_q = sT.n
                    em("vector", lambda e, c0=c0: e.tensor_copy(qslot(0, c0, 512, 0, 64),
                                                                psA[0:64, 512:1024]),
                       waits=[(sT, t_q)] + rwait("Qc"), inc=sV)
                    em("vector", lambda e, c0=c0: e.tensor_copy(qslot(1, c0, 512, 64, 64),
                                                                psA[64:128, 512:1024]), inc=sV)
                    consume(("psA", 1), sV)
                v_cont = sV.n
                for sl in range(2):
                    hh = h0 + sl
                    for tc in range(16):
                        for rch in range(2):
                            em("tensor", lambda e, tc=tc, rch=rch, hh=hh:
                               e.matmul(psA[:, 64 * tc:64 * (tc + 1)],
                                        kvn_ap(rch, 128 * tc, 128),
                                        wvdec_sb[:, 1024 * rch + 64 * hh: 1024 * rch + 64 * (hh + 1)],
                                        start=(rch == 0), stop=(rch == 1)),
                               waits=(rwait(("psA", 0), ("psA", 1))) if (tc == 0 and rch == 0) else [],
                               inc=(sT if (rch == 1 and tc == 15) else None))
                    t_v = sT.n
                    voff = 64 if sl == 0 else 64
                    dcol = 0 if sl == 0 else 64
                    em("vector", lambda e, sl=sl, dcol=dcol:
                       e.tensor_copy(
                           UP[0:128, OFF_VR + sl * 2048: OFF_VR + (sl + 1) * 2048]
                           .rearrange("p (k c) -> p k c", c=128)[:, :, dcol:dcol + 64],
                           psA[:, 0:1024].rearrange("p (k c) -> p k c", c=64)),
                       waits=[(sT, t_v)] + rwait("Vs"), inc=sV)
                    consume(("psA", 0), sV)
                    consume(("psA", 1), sV)
                v_vdec = sV.n
                pa = 16 * (h0 % 8)
                half = h0 // 8
                for idx, base_off in ((0, OFF_KR), (1, OFF_KR), (2, OFF_QRG), (3, OFF_QRG)):
                    rr_e = 65 if idx % 2 == 0 else 81
                    rr_o = 0 if idx % 2 == 0 else 16
                    dma("gpsimd", UP[rr_e:rr_e + 16, base_off: base_off + T],
                        rope_ap(idx, pa, 16, half * T, T), sG,
                        waits=[(sV, v_proj)] + rwait("Kc" if base_off == OFF_KR else "Qc"))
                    dma("gpsimd", UP[rr_o:rr_o + 16, base_off + T: base_off + 2 * T],
                        rope_ap(idx, pa + 16, 16, half * T, T), sG)
                g_rope = sG.n
                return v_cont, v_vdec, g_rope

            def bG_chain(sl, hh, v_ready, g_ready):
                par = hh % 2
                prow = 64 if par == 0 else 32
                ocol = ocol_e if par == 0 else ocol_o
                em("vector", lambda e, sl=sl: e.tensor_tensor(ksq[:], kslot(sl, 0, T),
                                                              kslot(sl, 0, T), ALU.mult),
                   waits=[(sV, v_ready), (sG, g_ready), (sV, v_preset)] + rwait("ksq"), inc=sV)
                v_ksq = sV.n
                consume_add("Kc", sV)
                for blk in range(4):
                    em("tensor", lambda e, blk=blk, prow=prow:
                       e.matmul(psC[prow:prow + 1, 0:512], onescol[:],
                                ksq[:, 512 * blk:512 * (blk + 1)], start=True, stop=True),
                       waits=[(sV, v_ksq)] + rwait("psC"), inc=sT)
                    t_g = sT.n
                    em("vector", lambda e, blk=blk, prow=prow:
                       e.reduce_max(gcol[prow:prow + 1, 4 + blk:5 + blk],
                                    psC[prow:prow + 1, 0:512], AXX),
                       waits=[(sT, t_g)], inc=sV)
                    consume("psC", sV)
                em("vector", lambda e, prow=prow:
                   e.reduce_max(gcol[prow:prow + 1, 0:1], gcol[prow:prow + 1, 4:8], AXX), inc=sV)
                v_G = sV.n
                em("vector", lambda e, sl=sl: e.tensor_tensor(ksq[:], qslot(sl, 0, T),
                                                              qslot(sl, 0, T), ALU.mult),
                   waits=[(sV, v_ready), (sG, g_ready)] + rwait("ksq"), inc=sV)
                v_qsq = sV.n
                consume_add("Qc", sV)
                for qb in range(4):
                    em("tensor", lambda e, qb=qb, prow=prow, ocol=ocol:
                       e.matmul(psB[prow:prow + 1, 512 * qb:512 * (qb + 1)], ocol[:],
                                ksq[:, 512 * qb:512 * (qb + 1)], start=True, stop=True),
                       waits=([(sV, v_qsq)] + rwait("psBav", "psBrow")) if qb == 0 else [], inc=sT)
                t_b = sT.n
                consume("ksq", sT)
                em("scalar", lambda e, prow=prow:
                   e.activation(ksq[prow:prow + 1, 0:T], psB[prow:prow + 1, 0:T], AF.Ln,
                                scale=gcol[prow:prow + 1, 0:1], bias=cc[prow:prow + 1, 4:5]),
                   waits=[(sT, t_b), (sV, v_G)], inc=sS)
                em("scalar", lambda e, sl=sl, prow=prow:
                   e.activation(qslot(sl, 0, T, prow, 1), ksq[prow:prow + 1, 0:T],
                                AF.Exp, scale=0.5), inc=sS)
                consume("psBrow", sS)
                consume_add("ksq", sS)
                return sS.n

            def inner_head(sl, hh, s_b, kt_ctr):
                par = hh % 2
                lrow = 64 if par == 0 else 0
                drow = 0 if par == 0 else 64
                av_last = 0
                first_av = True
                for qb in range(4):
                    q0 = 512 * qb
                    for kt in range(4 * qb + 4):
                        ring = kt_ctr[0] % 3
                        kt_ctr[0] += 1
                        qs = max(q0, 128 * kt)
                        n = 512 * (qb + 1) - qs
                        diag = (128 * kt >= q0)
                        em("tensor", lambda e, sl=sl, kt=kt, qs=qs, n=n, ring=ring, diag=diag:
                           e.matmul(psA[:, 512 * ring: 512 * ring + n],
                                    kslot(sl, 128 * kt, 128), qslot(sl, qs, n),
                                    start=True, stop=not diag),
                           waits=[(sS, s_b)] + rwait(("psA", ring)),
                           inc=(None if diag else sT))
                        if diag:
                            em("tensor", lambda e, ring=ring:
                               e.matmul(psA[:, 512 * ring: 512 * ring + 128],
                                        atri[:], btri[:, 0:128], start=False, stop=True),
                               inc=sT)
                        t_sc = sT.n
                        em("scalar", lambda e, ring=ring, n=n:
                           e.activation(attnT[:, 512 * ring: 512 * ring + n],
                                        psA[:, 512 * ring: 512 * ring + n], AF.Exp),
                           waits=[(sT, t_sc)] + rwait(("attnT", ring)), inc=sS)
                        consume(("psA", ring), sS)
                        s_exp = sS.n
                        em("tensor", lambda e, sl=sl, kt=kt, qs=qs, n=n, ring=ring,
                           last=(qb == 3 and kt == 4 * qb + 3):
                           e.matmul(psB[:, qs:qs + n], vslot(sl, 128 * kt, 128),
                                    attnT[:, 512 * ring: 512 * ring + n],
                                    start=(kt == 0), stop=last),
                           waits=[(sS, s_exp)] + (rwait("psBav") if first_av else []),
                           inc=sT)
                        first_av = False
                        consume(("attnT", ring), sT)
                        av_last = sT.n
                for lh in range(2):
                    em("scalar", lambda e, lrow=lrow, lh=lh:
                       e.activation(rows[lrow:lrow + 1, 0:1024],
                                    psB[lrow:lrow + 1, 1024 * lh:1024 * (lh + 1)],
                                    AF.Copy, bias=0.0),
                       waits=[(sT, av_last)] + rwait("lrow"), inc=sS)
                    s_l = sS.n
                    dma("gpsimd", Lbuf[hh:hh + 1, 1024 * lh:1024 * (lh + 1)],
                        rows[lrow:lrow + 1, 0:1024], sG,
                        waits=[(sS, s_l)] + rwait("Lbufreg"))
                    consume("lrow", sG)
                ch = hh // 2
                em("vector", lambda e, ch=ch, drow=drow:
                   e.tensor_copy(outT_ap(ch, 0, T, drow, 64), psB[drow:drow + 64, 0:T]),
                   waits=[(sT, av_last)], inc=sV)
                consume("psBav", sV)
                consume_add("psBav", sG)

            emit_ring_presets([(sW, sW.n)])
            kt_ctr = [0]
            for pair in range(8):
                v_cont, v_vdec, g_rope = prep_pair(pair)
                sb0 = bG_chain(0, 2 * pair, v_cont, g_rope)
                sb1 = bG_chain(1, 2 * pair + 1, v_cont, g_rope)
                if debug and s == 0 and pair < 2:
                    for sl in range(2):
                        dma("sync", dbg["kring"][2 * pair + sl], kslot(sl, 0, T), sO,
                            waits=[(sS, sb1), (sV, v_vdec)])
                        dma("sync", dbg["qring"][2 * pair + sl], qslot(sl, 0, T), sO,
                            waits=[(sS, sb1), (sV, v_vdec)])
                    if pair == 0:
                        for sl in range(2):
                            dma("sync", dbg["vring"][sl], vslot(sl, 0, 2048), sO,
                                waits=[(sV, v_vdec)])
                inner_head(0, 2 * pair, sb0, kt_ctr)
                inner_head(1, 2 * pair + 1, sb1, kt_ctr)
                consume_add("Kc", sT)
                consume_add("Qc", sT)
                consume_add("Vs", sT)
            consume("kvnreg", sT)
            consume("qcreg", sT)
            consume("ropereg", sG)

            v_lcl = em("vector", lambda e: e.tensor_scalar(Lbuf[:], Lbuf[:], 1e-19, None, ALU.max),
                       waits=[(sG, sG.n)], inc=sV)
            em("scalar", lambda e: e.activation(Lbuf[:], Lbuf[:], AF.Ln),
               waits=[(sV, v_lcl)], inc=sS)
            em("scalar", lambda e: e.activation(LbufR[:], Lbuf[:], AF.Exp, scale=-1.0),
               waits=rwait("LbufRreg"), inc=sS)
            s_lr = sS.n
            consume("Lbufreg", sS)
            if debug:
                dma("sync", dbg["lrows"][s], LbufR[:], sO, waits=[(sS, s_lr)])
            for ch in range(8):
                g_rl = dma("gpsimd", rl2[:], LbufR[2 * ch:2 * ch + 2, :], sG,
                           waits=[(sS, s_lr)] + rwait("rl2", "ksq"))
                for tb in range(4):
                    em("tensor", lambda e, tb=tb:
                       e.matmul(psC[:, 0:512], e2sel[:], rl2[:, 512 * tb:512 * (tb + 1)],
                                start=True, stop=True),
                       waits=[(sG, g_rl)] + rwait("psC"), inc=sT)
                    t_n = sT.n
                    em("vector", lambda e, ch=ch, tb=tb:
                       e.tensor_tensor(outT_ap(ch, 512 * tb, 512), outT_ap(ch, 512 * tb, 512),
                                       psC[:, 0:512], ALU.mult),
                       waits=[(sT, t_n)], inc=sV)
                    consume("psC", sV)
                consume("rl2", sV)
                consume_add("ksq", sG)
            v_norm = sV.n
            consume("LbufRreg", sG)
            if debug:
                for ch in range(8):
                    dma("sync", dbg["outT"][s][128 * ch:128 * (ch + 1), :], outT_ap(ch, 0, T),
                        sO, waits=[(sV, v_norm)])
            wo_sb = WP1[:, 0:8192]
            for ch in range(8):
                dma("gpsimd", wo_sb[:, 1024 * ch:1024 * (ch + 1)],
                    wo_d[128 * ch:128 * (ch + 1), :], sG, waits=rwait("WP1"))
            g_wo = sG.n
            for tb in range(4):
                c0 = 512 * tb
                for jc in range(8):
                    blk = jc % 4
                    for ch in range(8):
                        em("tensor", lambda e, jc=jc, ch=ch, c0=c0, blk=blk:
                           e.matmul(psB[:, 512 * blk:512 * (blk + 1)],
                                    wo_sb[:, 1024 * ch + 128 * jc: 1024 * ch + 128 * (jc + 1)],
                                    outT_ap(ch, c0, 512), start=(ch == 0), stop=(ch == 7)),
                           waits=([(sG, g_wo), (sV, v_norm)] + rwait(("psB", blk), "psBav"))
                           if ch == 0 else [],
                           inc=(sT if ch == 7 else None))
                    t_p = sT.n
                    em("vector", lambda e, jc=jc, blk=blk:
                       e.tensor_copy(projs[:, 512 * jc:512 * (jc + 1)],
                                     psB[:, 512 * blk:512 * (blk + 1)]),
                       waits=[(sT, t_p)] + (rwait("projs") if jc == 0 else []), inc=sV)
                    em("vector", lambda e, jc=jc:
                       e.tensor_tensor(ffout[:, 512 * jc:512 * (jc + 1)],
                                       projs[:, 512 * jc:512 * (jc + 1)],
                                       projs[:, 512 * jc:512 * (jc + 1)], ALU.mult),
                       waits=(rwait("ffout") if jc == 0 else []), inc=sV)
                    consume(("psB", blk), sV)
                v_pe = sV.n
                for ch in range(8):
                    em("tensor", lambda e, ch=ch:
                       e.matmul(psC[0:1, 0:512], onescol[:], projs[:, 512 * ch:512 * (ch + 1)],
                                start=(ch == 0), stop=(ch == 7)),
                       waits=([(sV, v_pe)] + rwait("psC")) if ch == 0 else [],
                       inc=(sT if ch == 7 else None))
                t_sum = sT.n
                for ch in range(8):
                    em("tensor", lambda e, ch=ch:
                       e.matmul(psC[32:33, 0:512], onescol[:], ffout[:, 512 * ch:512 * (ch + 1)],
                                start=(ch == 0), stop=(ch == 7)),
                       waits=[(sV, v_pe)] if ch == 0 else [], inc=(sT if ch == 7 else None))
                t_ssq = sT.n
                consume_add("projs", sT)
                consume_add("ffout", sT)
                s_rstd, v_Brow = emit_ln_rows(t_sum, t_ssq)
                t_bc = emit_AB_bcast(s_rstd, v_Brow)
                ap_in = xin[s].rearrange("(c p) t -> p c t", p=128)[:, :, c0:c0 + 512]
                g_x = dma("gpsimd", xbstg[:].rearrange("p (c t) -> p c t", c=8), ap_in, sG,
                          waits=rwait("xbstg"))
                for ch in range(8):
                    em("vector", lambda e, ch=ch:
                       e.tensor_tensor(stgA[:, 0:512], projs[:, 512 * ch:512 * (ch + 1)],
                                       psA[:, 0:512], ALU.mult),
                       waits=[(sT, t_bc), (sG, g_x)] if ch == 0 else [], inc=sV)
                    em("vector", lambda e:
                       e.tensor_tensor(stgB[:, 0:512], stgA[:, 0:512], psA[:, 512:1024], ALU.add),
                       inc=sV)
                    em("vector", lambda e, ch=ch:
                       e.tensor_tensor(outstg[:, 0:512], stgB[:, 0:512],
                                       xbstg[:, 512 * ch:512 * (ch + 1)], ALU.add),
                       waits=rwait("outstg"), inc=sV)
                    v_x1 = sV.n
                    dma("sync", x1d[s][128 * ch:128 * (ch + 1), c0:c0 + 512],
                        outstg[:, 0:512], sW, waits=[(sV, v_x1)])
                    consume("outstg", sW)
                consume("xbstg", sV)
                consume_add("projs", sV)
                consume(("psA", 0), sV)
                consume(("psA", 1), sV)
            consume("WP1", sT)
            consume("xlnreg", sT)
            x1_wr = sW.n
            if debug:
                for ch in range(8):
                    dma("sync", dbg["x1"][s][128 * ch:128 * (ch + 1), :],
                        x1d[s][128 * ch:128 * (ch + 1), :], sO, waits=[(sW, x1_wr)])

        def ffo_ap(jc, c0, n):
            if jc < 4:
                return ffout[:, 1024 * jc + c0: 1024 * jc + c0 + n]
            if jc < 6:
                return cosT[:, 1024 * (jc - 4) + c0: 1024 * (jc - 4) + c0 + n]
            return sinT[:, 1024 * (jc - 6) + c0: 1024 * (jc - 6) + c0 + n]

        def emit_qscale(nsub, subs, deq, v_tree, dbg_wait=0, dbg_tap=False, base=0):
            RAW0, LN0, SC0, DQ0 = base, base + 10, base + 20, base + 30
            n = len(subs)

            def fq(reg):
                return psC[:, 0:128] if reg == 0 else psB[:, 1536:1664]
            for i in range(n):
                sc_, sn_ = subs[i]
                for rep in range(2 if i == n - 1 else 1):
                    em("tensor", lambda e, sc_=sc_, sn_=sn_, reg=i % 2:
                       e.matmul(fq(reg)[0:sn_, 0:128], stgD[:, sc_:sc_ + sn_], identb[:],
                                start=True, stop=True),
                       waits=[(sV, v_tree)] + rwait(("psCq", i % 2))
                       + (rwait(("psB", 3), "psBav") if i == 1 else [])
                       + ([(sO, dbg_wait)] if (dbg_wait and i == 0) else []),
                       inc=sT)
                t_tr = sT.n
                em("vector", lambda e, i=i, reg=i % 2:
                   e.reduce_max(mxcol[:, RAW0 + i:RAW0 + i + 1],
                                fq(reg)[0:128, 0:128], AXX),
                   waits=[(sT, t_tr)], inc=sV)
                consume(("psCq", i % 2), sV)
            em("vector", lambda e, nsub=nsub:
               e.tensor_scalar(mxcol[:, RAW0:RAW0 + nsub], mxcol[:, RAW0:RAW0 + nsub],
                               1e-10, None, ALU.max), inc=sV)
            if dbg_tap:
                lreg = (n - 1) % 2
                dma("sync", dbg["stgd2"], stgD[:, 0:1026], sO, waits=[(sT, t_tr)])
                em("vector", lambda e, lreg=lreg:
                   e.tensor_copy(stgA[:, 0:128], fq(lreg)[0:128, 0:128]), inc=sV)
                vps = sV.n
                dma("sync", dbg["psdump"], stgA[:, 0:128], sO, waits=[(sV, vps)])
                em("vector", lambda e, lreg=lreg:
                   e.reduce_max(mxcol[:, 89:90], fq(lreg)[0:128, 0:128], AXX), inc=sV)
            v_mx = sV.n
            em("scalar", lambda e, nsub=nsub:
               e.activation(mxcol[:, LN0:LN0 + nsub], mxcol[:, RAW0:RAW0 + nsub], AF.Ln),
               waits=[(sV, v_mx)], inc=sS)
            em("scalar", lambda e, nsub=nsub:
               e.activation(mxcol[:, SC0:SC0 + nsub], mxcol[:, LN0:LN0 + nsub], AF.Exp,
                            scale=-0.5, bias=cc[:, 2:3]), inc=sS)
            if deq:
                em("scalar", lambda e, nsub=nsub:
                   e.activation(mxcol[:, DQ0:DQ0 + nsub], mxcol[:, LN0:LN0 + nsub], AF.Exp,
                                scale=0.5, bias=gcol[:, 3:4]), inc=sS)
            for _ in range(2):
                em("vector", lambda e, c=RAW0 + n - 1, reg=(n - 1) % 2:
                   e.reduce_max(mxcol[:, c:c + 1], fq(reg)[0:128, 0:128], AXX),
                   inc=sV)
            consume(("psCq", (n - 1) % 2), sV)
            em("vector", lambda e, c=RAW0 + n - 1:
               e.tensor_scalar(mxcol[:, c:c + 1], mxcol[:, c:c + 1],
                               1e-10, None, ALU.max), inc=sV)
            v_mx2 = sV.n
            em("scalar", lambda e, c=n - 1:
               e.activation(mxcol[:, LN0 + c:LN0 + c + 1], mxcol[:, RAW0 + c:RAW0 + c + 1],
                            AF.Ln), waits=[(sV, v_mx2)], inc=sS)
            em("scalar", lambda e, c=n - 1:
               e.activation(mxcol[:, SC0 + c:SC0 + c + 1], mxcol[:, LN0 + c:LN0 + c + 1],
                            AF.Exp, scale=-0.5, bias=cc[:, 2:3]), inc=sS)
            if deq:
                em("scalar", lambda e, c=n - 1:
                   e.activation(mxcol[:, DQ0 + c:DQ0 + c + 1], mxcol[:, LN0 + c:LN0 + c + 1],
                                AF.Exp, scale=0.5, bias=gcol[:, 3:4]), inc=sS)
            s_sc = sS.n
            for r in range(2 if deq else 1):
                cbase = SC0 if r == 0 else DQ0
                rc0 = 512 * r
                def br(reg):
                    return psC[0:1, 0:128] if reg == 0 else psB[0:1, 1536:1664]
                for j in range(n):
                    sc_, sn_ = subs[j]
                    for rep in range(2 if j == n - 1 else 1):
                        em("tensor", lambda e, cbase=cbase, j=j, reg=j % 2:
                           e.matmul(br(reg), mxcol[:, cbase + j:cbase + j + 1], identf[:],
                                    start=True, stop=True),
                           waits=[(sS, s_sc)] + rwait(("psCr", j % 2), ("psCq", j % 2))
                           + (rwait(("psB", 3), "psBav") if (j == 1 and r == 0) else []),
                           inc=sT)
                    t_tb = sT.n
                    em("vector", lambda e, rc0=rc0, sn_=sn_, reg=j % 2:
                       e.tensor_copy(rwsB[0:1, rc0:rc0 + sn_], br(reg)[0:1, 0:sn_]),
                       waits=[(sT, t_tb)], inc=sV)
                    if j == n - 1:
                        em("vector", lambda e, rc0=rc0, sn_=sn_, reg=j % 2:
                           e.tensor_copy(rwsB[0:1, rc0:rc0 + sn_], br(reg)[0:1, 0:sn_]),
                           inc=sV)
                    consume(("psCr", j % 2), sV)
                    v_row = sV.n
                    tgt = psB if r == 0 else psA
                    if sc_ < 1024 < sc_ + sn_:
                        pieces = [(sc_, 0, 1024 - sc_), (1024, 1024 - sc_, sc_ + sn_ - 1024),
                                  (sc_, 0, 1024 - sc_)]
                    else:
                        pieces = [(sc_, 0, sn_)] * (2 if j == n - 1 else 1)
                    for pi, (pc, po, pw) in enumerate(pieces):
                        em("tensor", lambda e, pc=pc, po=po, pw=pw, tgt=tgt, rc0=rc0:
                           e.matmul(tgt[:, pc:pc + pw], onesrow[0:1, 0:128],
                                    rwsB[0:1, rc0 + po:rc0 + po + pw], start=True, stop=True),
                           waits=[(sV, v_row)] + ((rwait(("psB", 0), ("psB", 1), ("psB", 2),
                                                         ("psB", 3), "psBav") if r == 0 else
                                                   rwait(("psA", 0), ("psA", 1), ("psA", 2)))
                                                  if (j == 0 and pi == 0) else []),
                           inc=sT)

            t_ret = sT.n
            if dbg_tap:
                dma("sync", dbg["mxh"], mxcol[:], sO, waits=[(sS, s_sc)])
                em("vector", lambda e: e.tensor_copy(ffout[:, 0:1024], psB[:, 0:1024]),
                   waits=[(sT, t_ret)], inc=sV)
                vv = sV.n
                dd = dma("sync", dbg["sbch"], ffout[:, 0:1024], sO, waits=[(sV, vv)])
                em("tensor", lambda e: None, waits=[(sO, dd)])
            return t_ret

        def hbuf_ap(ch, c0, n):
            return up(0, 128, OFF_HBUF + ch * 1026 + c0, n)

        def x1b_ap(ch, c0, n):
            return up(0, 128, OFF_X1B + ch * 1026 + c0, n)

        def x2ln_ap(ch, c0, n):
            return up(0, 128, OFF_X2LN + ch * 1026 + c0, n)

        def xq_ap(ch, c0, n):
            return up(0, 128, OFF_XQ + ch * 1026 + c0, n)

        LN2BLOCKS = [(0, 512), (512, 512), (514, 512)]
        W1BLOCKS = [(1024, 2), (0, 512), (512, 512)]
        x1_final_wr = sW.n

        for s in range(NSEQ):
            for ft in range(NFT):
                t0 = ft * FTILE
                lo = t0 - 1
                n_lo = 1 if t0 == 0 else 0
                hi = t0 + FTILE + 1
                n_hi = 1 if hi > T else 0
                src_lo = max(lo, 0)
                src_n = (FTILE + 2) - n_lo - n_hi
                g_x1 = dma("gpsimd",
                           UP[0:128, OFF_X1B: OFF_X1B + 8 * 1026].rearrange(
                               "p (c t) -> p c t", c=8)[:, :, n_lo:n_lo + src_n],
                           x1d[s].rearrange("(c p) t -> p c t", p=128)[:, :, src_lo:src_lo + src_n],
                           sG, waits=[(sW, x1_final_wr if (s == 0 and ft == 0) else x1_final_wr)]
                           + rwait("x1b"))
                v_halo = sV.n
                if n_lo:
                    v_halo = em("vector", lambda e: e.memset(
                        UP[0:128, OFF_X1B: OFF_X1B + 8 * 1026].rearrange(
                            "p (c t) -> p c t", c=8)[:, :, 0:1], 0.0),
                        waits=rwait("x1b"), inc=sV)
                if n_hi:
                    v_halo = em("vector", lambda e: e.memset(
                        UP[0:128, OFF_X1B: OFF_X1B + 8 * 1026].rearrange(
                            "p (c t) -> p c t", c=8)[:, :, 1025:1026], 0.0),
                        waits=rwait("x1b"), inc=sV)

                for (b0, bn) in LN2BLOCKS:
                    for ch in range(8):
                        em("vector", lambda e, ch=ch, b0=b0, bn=bn:
                           e.tensor_tensor(projs[:, 512 * ch:512 * ch + bn], x1b_ap(ch, b0, bn),
                                           x1b_ap(ch, b0, bn), ALU.mult),
                           waits=([(sG, g_x1), (sV, v_halo)] + rwait("projs")) if ch == 0 else [],
                           inc=sV)
                    v_s = sV.n
                    for ch in range(8):
                        em("tensor", lambda e, ch=ch, b0=b0, bn=bn:
                           e.matmul(psC[0:1, 0:bn], onescol[:], x1b_ap(ch, b0, bn),
                                    start=(ch == 0), stop=(ch == 7)),
                           waits=([(sG, g_x1), (sV, v_halo)] + rwait("psC")) if ch == 0 else [],
                           inc=(sT if ch == 7 else None))
                    t_sum = sT.n
                    for ch in range(8):
                        em("tensor", lambda e, ch=ch, bn=bn:
                           e.matmul(psC[32:33, 0:bn], onescol[:], projs[:, 512 * ch:512 * ch + bn],
                                    start=(ch == 0), stop=(ch == 7)),
                           waits=[(sV, v_s)] if ch == 0 else [],
                           inc=(sT if ch == 7 else None))
                    t_ssq = sT.n
                    consume_add("projs", sT)
                    s_rstd, v_Brow = emit_ln_rows(t_sum, t_ssq, bn)
                    t_bc = emit_AB_bcast(s_rstd, v_Brow, bn)
                    for ch in range(8):
                        em("vector", lambda e, ch=ch, b0=b0, bn=bn:
                           e.tensor_tensor(stgA[:, 0:bn], x1b_ap(ch, b0, bn), psA[:, 0:bn],
                                           ALU.mult),
                           waits=[(sT, t_bc)] if ch == 0 else [], inc=sV)
                        em("vector", lambda e, ch=ch, b0=b0, bn=bn:
                           e.tensor_tensor(x2ln_ap(ch, b0, bn), stgA[:, 0:bn],
                                           psA[:, 512:512 + bn], ALU.add), inc=sV)
                    consume(("psA", 0), sV)
                    consume(("psA", 1), sV)
                v_x2 = sV.n

                for ch in range(8):
                    em("vector", lambda e, ch=ch: e.tensor_tensor(
                        stgC[:, 0:1026], x2ln_ap(ch, 0, 1026), x2ln_ap(ch, 0, 1026), ALU.mult),
                        waits=([(sV, v_x2)] + rwait("stgCD")) if ch == 0 else [], inc=sV)
                    if ch == 0:
                        em("vector", lambda e: e.tensor_copy(stgD[:, 0:1026], stgC[:, 0:1026]),
                           inc=sV)
                    else:
                        em("vector", lambda e: e.tensor_tensor(stgD[:, 0:1026], stgD[:, 0:1026],
                                                               stgC[:, 0:1026], ALU.max), inc=sV)
                v_tree = sV.n
                subs = [(i * 128, 128) for i in range(8)] + [(898, 128)]
                t_bcq = emit_qscale(9, subs, True, v_tree)
                em("vector", lambda e:
                   e.tensor_copy(stgC[:, 0:1026], psB[:, 0:1026]),
                   waits=[(sT, t_bcq)] + rwait("stgCD"), inc=sV)
                em("vector", lambda e:
                   e.tensor_copy(stgC[:, 0:1026], psA[:, 0:1026]), inc=sV)
                for ch in range(8):
                    em("vector", lambda e, ch=ch:
                       e.tensor_tensor(stgC[:, 0:1026], x2ln_ap(ch, 0, 1026), psB[:, 0:1026],
                                       ALU.mult), inc=sV)
                    em("vector", lambda e: e.tensor_scalar(stgI[:, 0:1026], stgC[:, 0:1026],
                                                           256.5, None, ALU.add), inc=sV)
                    em("vector", lambda e:
                       e.scalar_tensor_tensor(stgD[:, 0:1026], stgI[:, 0:1026], -256.5,
                                              stgC[:, 0:1026], ALU.add, ALU.is_gt), inc=sV)
                    em("vector", lambda e:
                       e.scalar_tensor_tensor(stgC[:, 0:1026], stgI[:, 0:1026], -256.0,
                                              stgD[:, 0:1026], ALU.add, ALU.subtract), inc=sV)
                    em("vector", lambda e, ch=ch:
                       e.tensor_tensor(xq_ap(ch, 0, 1026), stgC[:, 0:1026], psA[:, 0:1026],
                                       ALU.mult), inc=sV)
                v_xq = sV.n
                for b_ in (("psB", 0), ("psB", 1), ("psB", 2), ("psA", 0), ("psA", 1), ("psA", 2)):
                    consume(b_, sV)
                consume("stgCD", sV)
                if debug and s == 0 and ft == 0:
                    for ch in range(8):
                        dma("sync", dbg["xq"][0][128 * ch:128 * (ch + 1), :], xq_ap(ch, 0, 1026),
                            sO, waits=[(sV, v_xq)])

                s_sil_last = 0
                for hg in range(8):
                    wbuf = hg % 2
                    w1b = WP2[:, 4096 * wbuf: 4096 * (wbuf + 1)]
                    g_w1 = dma("sync", w1b,
                               w1q_s[:, 4096 * hg:4096 * (hg + 1)],
                               sW, waits=rwait(("w1b", wbuf), "WP2") + [(sG, g_wq_done)])
                    for hsub in range(4):
                        hc = 4 * hg + hsub
                        for bi, (b0, bn) in enumerate(W1BLOCKS):
                            for dc in range(8):
                                em("tensor", lambda e, hsub=hsub, dc=dc, b0=b0, bn=bn, w1b=w1b:
                                   e.matmul(psA[:, b0:b0 + bn],
                                            w1b[:, 512 * dc + 128 * hsub: 512 * dc + 128 * (hsub + 1)],
                                            xq_ap(dc, b0, bn), start=(dc == 0), stop=(dc == 7)),
                                   waits=([(sW, g_w1), (sV, v_xq)]
                                          + rwait(("psA", 0), ("psA", 1), ("psA", 2)))
                                   if (dc == 0 and bi == 0) else [],
                                   inc=(sT if (dc == 7 and bi == 2) else None))
                        t_h = sT.n
                        em("vector", lambda e: e.tensor_copy(stgC[:, 0:1026], psA[:, 0:1026]),
                           waits=[(sT, t_h)] + rwait("stgCD"), inc=sV)
                        for b_ in (("psA", 0), ("psA", 1), ("psA", 2)):
                            consume(b_, sV)
                        em("vector", lambda e, hc=hc:
                           e.tensor_scalar(stgD[:, 0:1024], stgC[:, 0:1024],
                                           convw[:, 3 * hc:3 * hc + 1], convb[:, hc:hc + 1],
                                           ALU.mult, ALU.add), inc=sV)
                        em("vector", lambda e, hc=hc:
                           e.scalar_tensor_tensor(stgD[:, 0:1024], stgC[:, 1:1025],
                                                  convw[:, 3 * hc + 1:3 * hc + 2], stgD[:, 0:1024],
                                                  ALU.mult, ALU.add), inc=sV)
                        em("vector", lambda e, hc=hc:
                           e.scalar_tensor_tensor(stgD[:, 0:1024], stgC[:, 2:1026],
                                                  convw[:, 3 * hc + 2:3 * hc + 3], stgD[:, 0:1024],
                                                  ALU.mult, ALU.add), inc=sV)
                        v_cv = sV.n
                        em("scalar", lambda e, hc=hc:
                           e.activation(hbuf_ap(hc, 0, 1024), stgD[:, 0:1024], AF.Silu),
                           waits=[(sV, v_cv)] + (rwait("hbufr") if hc == 0 else []), inc=sS)
                        consume_add("stgCD", sS)
                        s_sil_last = sS.n
                    consume(("w1b", wbuf), sT)

                dbg_hs = 0
                if debug and s == 0 and ft == 0:
                    for ch in range(32):
                        dbg_hs = dma("sync", dbg["hsil"][0][128 * ch:128 * (ch + 1), :],
                                     hbuf_ap(ch, 0, 1024), sO, waits=[(sS, s_sil_last)])
                for ch in range(32):
                    em("vector", lambda e, ch=ch: e.tensor_tensor(
                        stgC[:, 0:1024], hbuf_ap(ch, 0, 1024), hbuf_ap(ch, 0, 1024), ALU.mult),
                        waits=([(sS, s_sil_last)] + rwait("stgCD")
                               + ([(sO, dbg_hs)] if dbg_hs else [])) if ch == 0 else [], inc=sV)
                    if ch == 0:
                        em("vector", lambda e: e.tensor_copy(stgD[:, 0:1024], stgC[:, 0:1024]),
                           inc=sV)
                    else:
                        em("vector", lambda e: e.tensor_tensor(stgD[:, 0:1024], stgD[:, 0:1024],
                                                               stgC[:, 0:1024], ALU.max), inc=sV)
                v_tree = sV.n
                dbg_td = 0
                if debug and s == 0 and ft == 0:
                    dbg_td = dma("sync", dbg["treeD"], stgD[:, 0:1026], sO, waits=[(sV, v_tree)])
                subs_h = [(i * 128, 128) for i in range(8)]
                t_hbc = emit_qscale(8, subs_h, False, v_tree, dbg_wait=dbg_td, dbg_tap=(debug and s == 0 and ft == 0), base=40)
                em("vector", lambda e:
                   e.tensor_copy(stgC[:, 0:1024], psB[:, 0:1024]),
                   waits=[(sT, t_hbc)] + rwait("stgCD"), inc=sV)
                for ch in range(32):
                    em("vector", lambda e, ch=ch:
                       e.tensor_tensor(stgC[:, 0:1024], hbuf_ap(ch, 0, 1024), psB[:, 0:1024],
                                       ALU.mult), inc=sV)
                    em("vector", lambda e: e.tensor_scalar(stgI[:, 0:1024], stgC[:, 0:1024],
                                                           256.5, None, ALU.add), inc=sV)
                    em("vector", lambda e:
                       e.scalar_tensor_tensor(stgD[:, 0:1024], stgI[:, 0:1024], -256.5,
                                              stgC[:, 0:1024], ALU.add, ALU.is_gt), inc=sV)
                    em("vector", lambda e, ch=ch:
                       e.scalar_tensor_tensor(hbuf_ap(ch, 0, 1024), stgI[:, 0:1024], -256.0,
                                              stgD[:, 0:1024], ALU.add, ALU.subtract), inc=sV)
                v_hq = sV.n
                consume(("psB", 0), sV)
                consume(("psB", 1), sV)
                consume("stgCD", sV)
                if debug and s == 0 and ft == 0:
                    for ch in range(32):
                        dma("sync", dbg["hq"][0][128 * ch:128 * (ch + 1), :],
                            hbuf_ap(ch, 0, 1024), sO, waits=[(sV, v_hq)])

                for jc in range(8):
                    wbuf = jc % 2
                    w2b = WP1[:, 4096 * wbuf: 4096 * (wbuf + 1)]
                    g_w2 = dma("sync", w2b,
                               w2q_s[128 * jc:128 * (jc + 1), :],
                               sW, waits=rwait(("w2b", wbuf), "WP1") + [(sG, g_wq_done)])
                    pblk = 2 * (jc % 2)
                    for cb in range(2):
                        for hc in range(32):
                            em("tensor", lambda e, cb=cb, hc=hc, w2b=w2b, pblk=pblk:
                               e.matmul(psB[:, 512 * (pblk + cb): 512 * (pblk + cb + 1)],
                                        w2b[:, 128 * hc:128 * (hc + 1)],
                                        hbuf_ap(hc, 512 * cb, 512), start=(hc == 0), stop=(hc == 31)),
                               waits=([(sW, g_w2), (sV, v_hq)] + rwait(("psB", pblk + cb)))
                               if hc == 0 else [],
                               inc=(sT if hc == 31 else None))
                        t_f = sT.n
                        em("vector", lambda e, jc=jc, cb=cb, pblk=pblk:
                           e.tensor_copy(ffo_ap(jc, 512 * cb, 512),
                                         psB[:, 512 * (pblk + cb): 512 * (pblk + cb + 1)]),
                           waits=[(sT, t_f)] + (rwait("ffout") if (jc == 0 and cb == 0) else []),
                           inc=sV)
                        consume(("psB", pblk + cb), sV)
                    consume(("w2b", wbuf), sT)
                v_ff = sV.n
                consume("hbufr", sT)

                for cb in range(2):
                    c0 = 512 * cb
                    for jc in range(8):
                        em("vector", lambda e, jc=jc, c0=c0:
                           e.tensor_tensor(projs[:, 512 * jc:512 * (jc + 1)],
                                           ffo_ap(jc, c0, 512), ffo_ap(jc, c0, 512),
                                           ALU.mult),
                           waits=([(sV, v_ff)] + rwait("projs")) if jc == 0 else [], inc=sV)
                    v_sq = sV.n
                    for jc in range(8):
                        em("tensor", lambda e, jc=jc, c0=c0:
                           e.matmul(psC[0:1, 0:512], onescol[:], ffo_ap(jc, c0, 512),
                                    start=(jc == 0), stop=(jc == 7)),
                           waits=([(sV, v_ff)] + rwait("psC")) if jc == 0 else [],
                           inc=(sT if jc == 7 else None))
                    t_sum = sT.n
                    for jc in range(8):
                        em("tensor", lambda e, jc=jc:
                           e.matmul(psC[32:33, 0:512], onescol[:], projs[:, 512 * jc:512 * (jc + 1)],
                                    start=(jc == 0), stop=(jc == 7)),
                           waits=[(sV, v_sq)] if jc == 0 else [],
                           inc=(sT if jc == 7 else None))
                    t_ssq = sT.n
                    consume_add("projs", sT)
                    s_rstd, v_Brow = emit_ln_rows(t_sum, t_ssq)
                    t_bc = emit_AB_bcast(s_rstd, v_Brow)
                    for jc in range(8):
                        em("vector", lambda e, jc=jc, c0=c0:
                           e.tensor_tensor(stgA[:, 0:512], ffo_ap(jc, c0, 512),
                                           psA[:, 0:512], ALU.mult),
                           waits=[(sT, t_bc)] if jc == 0 else [], inc=sV)
                        em("vector", lambda e:
                           e.tensor_tensor(stgB[:, 0:512], stgA[:, 0:512], psA[:, 512:1024],
                                           ALU.add), inc=sV)
                        em("vector", lambda e, jc=jc, c0=c0:
                           e.tensor_tensor(outstg[:, 0:512], stgB[:, 0:512],
                                           x1b_ap(jc, c0 + 1, 512), ALU.add),
                           waits=rwait("outstg"), inc=sV)
                        v_o = sV.n
                        dma("sync", outd[s][128 * jc:128 * (jc + 1), t0 + c0: t0 + c0 + 512],
                            outstg[:, 0:512], sO, waits=[(sV, v_o)])
                        consume("outstg", sO)
                    consume(("psA", 0), sV)
                    consume(("psA", 1), sV)
                    consume_add("projs", sV)
                consume("ffout", sV)
                consume("x1b", sV)
                consume_add("x1b", sT)

        em("sync", lambda e: None, waits=[(sO, sO.n)])

        block = E(nc.Block())

        @block.sync
        def _(e):
            for f in Q["sync"]:
                f(e)

        @block.gpsimd
        def _(e):
            for f in Q["gpsimd"]:
                f(e)

        @block.tensor
        def _(e):
            for f in Q["tensor"]:
                f(e)

        @block.vector
        def _(e):
            for f in Q["vector"]:
                f(e)

        @block.scalar
        def _(e):
            for f in Q["scalar"]:
                f(e)

    return nc, dbg


_NC_CACHE = {}


def _host_consts():
    inv_freq = 1.0 / 10000.0 ** (np.arange(0, DR, 2, dtype=np.float32) / DR)
    t = np.arange(T, dtype=np.float32)
    freqs = np.outer(inv_freq, t)
    cosT = np.tile(np.cos(freqs), (8, 1)).astype(np.float32)
    sinT = np.tile(np.sin(freqs), (8, 1)).astype(np.float32)
    ident = np.eye(128, dtype=np.float32)
    onesr = np.ones((1, 128), np.float32)
    e2 = np.zeros((2, 128), np.float32)
    e2[0, 0:64] = 1.0
    e2[1, 64:128] = 1.0
    j = np.arange(128)
    atri = (j[:, None] < j[None, :]).astype(np.float32)
    btri = (-300.0 * (j[:, None] >= j[None, :])).astype(np.float32)
    ocole = np.ones((128, 1), np.float32)
    ocole[64] = 0.0
    ocole[97:] = 0.0
    ocolo = np.ones((128, 1), np.float32)
    ocolo[32:64] = 0.0
    d = dict(cosTc=cosT, sinTc=sinT, identc=ident, onesr=onesr, e2c=e2,
             atri=atri, btri=btri, ocole=ocole, ocolo=ocolo)
    out = {k: np.ascontiguousarray(v.astype(ml_dtypes.bfloat16)) for k, v in d.items()}
    out["identf"] = np.eye(128, dtype=np.float32)
    return out


def _rope_perm():
    perm = []
    for ab in range(2):
        for half in range(2):
            for hh in range(8):
                h = 8 * half + hh
                for jj in range(16):
                    perm.append(32 * h + 16 * ab + jj)
    return np.array(perm, dtype=np.int64)


def kernel(L0_attn_x, L0_attn_W_kv, L0_attn_g_kv, L0_attn_W_kdec, L0_attn_W_vdec,
           L0_attn_W_q, L0_attn_W_qdec, L0_attn_W_krope, L0_attn_W_qrope, L0_attn_W_o,
           L0_ffn_w1, L0_ffn_conv_w, L0_ffn_conv_b, L0_ffn_w2, _debug=0):
    x = np.ascontiguousarray(np.asarray(L0_attn_x, dtype=np.float32))
    ncores = 8
    perm = _rope_perm()
    consts = _host_consts()
    common = dict(
        wkv=np.ascontiguousarray(np.asarray(L0_attn_W_kv, np.float32)),
        gkv=np.ascontiguousarray(np.asarray(L0_attn_g_kv, np.float32).reshape(KVR, 1)),
        wq=np.ascontiguousarray(np.asarray(L0_attn_W_q, np.float32)),
        wkrope=np.ascontiguousarray(np.asarray(L0_attn_W_krope, np.float32)[:, perm]),
        wqrope=np.ascontiguousarray(np.asarray(L0_attn_W_qrope, np.float32)[:, perm]),
        wkdec=np.ascontiguousarray(np.asarray(L0_attn_W_kdec, np.float32)),
        wvdec=np.ascontiguousarray(np.asarray(L0_attn_W_vdec, np.float32)),
        wqdec=np.ascontiguousarray(np.asarray(L0_attn_W_qdec, np.float32)),
        wo=np.ascontiguousarray(np.asarray(L0_attn_W_o, np.float32)),
        w1t=np.ascontiguousarray(np.asarray(L0_ffn_w1, np.float32).T.reshape(
            8, 128, 8, 512).transpose(1, 2, 0, 3).reshape(128, 32768)),
        w2t=np.ascontiguousarray(np.asarray(L0_ffn_w2, np.float32).T.reshape(
            32, 128, 8, 128).transpose(2, 1, 0, 3).reshape(1024, 4096)),
        convw=np.ascontiguousarray(np.asarray(L0_ffn_conv_w, np.float32).reshape(FH, 3)),
        convb=np.ascontiguousarray(np.asarray(L0_ffn_conv_b, np.float32).reshape(FH, 1)),
        **consts,
    )
    key = ("prog", _debug)
    if key not in _NC_CACHE:
        _NC_CACHE[key] = build_program(debug=_debug)
    nc, dbg = _NC_CACHE[key]
    in_maps = []
    for c in range(ncores):
        m = dict(common)
        m["xin"] = np.ascontiguousarray(x[2 * c:2 * c + 2].transpose(0, 2, 1))
        in_maps.append(m)
    res = run_bass_kernel_spmd(nc, in_maps, core_ids=list(range(ncores)))
    outs = []
    for c in range(ncores):
        o = np.asarray(res.results[c]["outd"])
        outs.append(o.transpose(0, 2, 1))
    full = np.concatenate(outs, axis=0).astype(np.float32)
    kernel._last_results = res
    return full


# version 08 (final)
# speedup vs baseline: 1.0053x; 1.0053x over previous
import numpy as np
import ml_dtypes
import contextlib
import math
import concourse.bass as bass
import concourse.mybir as mybir
from concourse.bass_utils import run_bass_kernel_spmd

F32 = mybir.dt.float32
BF16 = mybir.dt.bfloat16
I16 = mybir.dt.int16
AF = mybir.ActivationFunctionType
ALU = mybir.AluOpType
AXX = mybir.AxisListType.X

NSEQ, T, D, H, DHD, DR, KVR, QR, FH = 2, 2048, 1024, 16, 64, 32, 256, 512, 4096
NTB = T // 512
ISQ96 = 1.0 / math.sqrt(96.0)
FTILE = 1024
NFT = T // FTILE
LN127 = float(np.log(127.0))


class Sem:
    def __init__(self, stk, nc, name):
        self.h = stk.enter_context(nc.semaphore(name))
        self.n = 0


def build_program(debug=0):
    nc = bass.Bass("TRN2", target_bir_lowering=False, detect_race_conditions=False)

    xin = nc.dram_tensor("xin", [NSEQ, D, T], F32, kind="ExternalInput").ap()
    wkv_d = nc.dram_tensor("wkv", [D, KVR], F32, kind="ExternalInput").ap()
    gkv_d = nc.dram_tensor("gkv", [KVR, 1], F32, kind="ExternalInput").ap()
    wq_d = nc.dram_tensor("wq", [D, QR], F32, kind="ExternalInput").ap()
    wkrope_d = nc.dram_tensor("wkrope", [D, QR], F32, kind="ExternalInput").ap()
    wqrope_d = nc.dram_tensor("wqrope", [QR, QR], F32, kind="ExternalInput").ap()
    wkdec_d = nc.dram_tensor("wkdec", [KVR, D], F32, kind="ExternalInput").ap()
    wvdec_d = nc.dram_tensor("wvdec", [KVR, D], F32, kind="ExternalInput").ap()
    wqdec_d = nc.dram_tensor("wqdec", [QR, D], F32, kind="ExternalInput").ap()
    wo_d = nc.dram_tensor("wo", [D, D], F32, kind="ExternalInput").ap()
    w1t_d = nc.dram_tensor("w1t", [128, 32768], F32, kind="ExternalInput").ap()
    w2t_d = nc.dram_tensor("w2t", [1024, 4096], F32, kind="ExternalInput").ap()
    convw_d = nc.dram_tensor("convw", [FH, 3], F32, kind="ExternalInput").ap()
    convb_d = nc.dram_tensor("convb", [FH, 1], F32, kind="ExternalInput").ap()
    cos_d = nc.dram_tensor("cosTc", [128, T], BF16, kind="ExternalInput").ap()
    sin_d = nc.dram_tensor("sinTc", [128, T], BF16, kind="ExternalInput").ap()
    ident_d = nc.dram_tensor("identc", [128, 128], BF16, kind="ExternalInput").ap()
    identf_d = nc.dram_tensor("identf", [128, 128], F32, kind="ExternalInput").ap()
    onesr_d = nc.dram_tensor("onesr", [1, 128], BF16, kind="ExternalInput").ap()
    e2_d = nc.dram_tensor("e2c", [2, 128], BF16, kind="ExternalInput").ap()
    atri_d = nc.dram_tensor("atri", [128, 128], BF16, kind="ExternalInput").ap()
    btri_d = nc.dram_tensor("btri", [128, 128], BF16, kind="ExternalInput").ap()
    ocole_d = nc.dram_tensor("ocole", [128, 1], BF16, kind="ExternalInput").ap()
    ocolo_d = nc.dram_tensor("ocolo", [128, 1], BF16, kind="ExternalInput").ap()
    x1d = nc.dram_tensor("x1d", [NSEQ, D, T], F32, kind="Internal").ap()
    w1q_s = nc.dram_tensor("w1q_s", [128, 32768], BF16, kind="Internal").ap()
    w2q_s = nc.dram_tensor("w2q_s", [1024, 4096], BF16, kind="Internal").ap()
    outd = nc.dram_tensor("outd", [NSEQ, D, T], F32, kind="ExternalOutput").ap()

    dbg = {}

    def dbg_t(name, shape, dt=F32):
        if debug:
            dbg[name] = nc.dram_tensor("dbg_" + name, shape, dt, kind="ExternalOutput").ap()

    dbg_t("x_ln", [NSEQ, D, T], BF16)
    dbg_t("kvn", [NSEQ, KVR, T], BF16)
    dbg_t("qc", [NSEQ, QR, T], BF16)
    dbg_t("ropek", [NSEQ, 2, 128, 2 * T], BF16)
    dbg_t("ropeq", [NSEQ, 2, 128, 2 * T], BF16)
    dbg_t("kring", [4, 128, T], BF16)
    dbg_t("qring", [4, 128, T], BF16)
    dbg_t("vring", [2, 128, 16 * 128], BF16)
    dbg_t("outT", [NSEQ, D, T], BF16)
    dbg_t("lrows", [NSEQ, 16, T], BF16)
    dbg_t("x1", [NSEQ, D, T])
    dbg_t("wrows", [3, 8])
    dbg_t("w1q", [128, 32768], BF16)
    dbg_t("w2q", [1024, 4096], BF16)
    dbg_t("xq", [1, D, 1026], BF16)
    dbg_t("hq", [1, FH, 1024], BF16)
    dbg_t("hsil", [1, FH, 1024], BF16)
    dbg_t("mxh", [128, 96])
    dbg_t("sbch", [128, 1024], BF16)
    dbg_t("treeD", [128, 1026], BF16)
    dbg_t("stgd2", [128, 1026], BF16)
    dbg_t("psdump", [128, 128])

    stk = contextlib.ExitStack()
    with stk:
        E = stk.enter_context

        UP = E(nc.sbuf_tensor("UP", [128, 57344], BF16))
        OFF_XLN, OFF_OUTT, OFF_KVN, OFF_QC = 0, 0, 16384, 20480
        OFF_ROPE = 28672
        OFF_KR, OFF_QRG, OFF_VR = 45056, 49152, 53248
        OFF_HBUF, OFF_X1B, OFF_X2LN = 0, 32832, 41040
        OFF_XQ = OFF_X2LN

        def up(p0, pn, off, cols):
            return UP[p0:p0 + pn, off: off + cols]

        WP1 = E(nc.sbuf_tensor("WP1", [128, 10240], BF16))
        WP2 = E(nc.sbuf_tensor("WP2", [128, 10240], BF16))
        cosT = E(nc.sbuf_tensor("cosT", [128, T], BF16))
        sinT = E(nc.sbuf_tensor("sinT", [128, T], BF16))
        identb = E(nc.sbuf_tensor("identb", [128, 128], BF16))
        onesrow = E(nc.sbuf_tensor("onesrow", [1, 128], BF16))
        onescol = E(nc.sbuf_tensor("onescol", [128, 1], BF16))
        ocol_e = E(nc.sbuf_tensor("ocol_e", [128, 1], BF16))
        ocol_o = E(nc.sbuf_tensor("ocol_o", [128, 1], BF16))
        e2sel = E(nc.sbuf_tensor("e2sel", [2, 128], BF16))
        atri = E(nc.sbuf_tensor("atri_s", [128, 128], BF16))
        btri = E(nc.sbuf_tensor("btri_s", [128, 128], BF16))
        gkv = E(nc.sbuf_tensor("gkvc", [128, 2], F32))
        convw = E(nc.sbuf_tensor("convwc", [128, 96], F32))
        convb = E(nc.sbuf_tensor("convbc", [128, 32], F32))
        Lbuf = E(nc.sbuf_tensor("Lbuf", [16, T], F32))
        LbufR = E(nc.sbuf_tensor("LbufR", [16, T], BF16))
        ksq = E(nc.sbuf_tensor("ksq", [128, T], BF16))
        rl2 = ksq[0:2, :]
        rows = E(nc.sbuf_tensor("rows", [128, 1024], F32))
        rwsB = E(nc.sbuf_tensor("rwsB", [128, 1024], BF16))
        onesrow_f = E(nc.sbuf_tensor("onesrow_f", [128, 128], F32))
        cc = E(nc.sbuf_tensor("cc", [128, 8], F32))
        gcol = E(nc.sbuf_tensor("gcol", [128, 8], F32))
        mxcol = E(nc.sbuf_tensor("mxcol", [128, 96], F32))
        identf = E(nc.sbuf_tensor("identf_s", [128, 128], F32))
        stgA = E(nc.sbuf_tensor("stgA", [128, 512], F32))
        stgB = E(nc.sbuf_tensor("stgB", [128, 512], F32))
        stgC = E(nc.sbuf_tensor("stgC", [128, 1026], BF16))
        stgD = E(nc.sbuf_tensor("stgD", [128, 1026], BF16))
        stgI = E(nc.sbuf_tensor("stgI", [128, 1026], I16))
        attnT = E(nc.sbuf_tensor("attnT", [128, 3 * 512], BF16))
        projs = E(nc.sbuf_tensor("projs", [128, 4096], BF16))

        def wqbuf(b):
            return UP[0:128, 2048 * b:2048 * (b + 1)].bitcast(F32)

        xbstg = UP[0:128, OFF_ROPE: OFF_ROPE + 4096]
        ffout = UP[0:128, 49248: 49248 + 4096]
        outstg = UP[0:128, 53344: 53344 + 1024].bitcast(F32)

        psA = E(nc.psum_tensor("psA", [128, 1536], F32))
        psB = E(nc.psum_tensor("psB", [128, 2048], F32))
        psC = E(nc.psum_tensor("psC", [128, 512], F32))

        sW = Sem(stk, nc, "sW")
        sG = Sem(stk, nc, "sG")
        sT = Sem(stk, nc, "sT")
        sV = Sem(stk, nc, "sV")
        sS = Sem(stk, nc, "sS")
        sO = Sem(stk, nc, "sO")

        Q = {k: [] for k in ("sync", "gpsimd", "tensor", "vector", "scalar")}
        REG = {}

        def em(eng, fn, waits=(), inc=None, inck=1):
            if inc is not None:
                inc.n += inck
            wl = []
            seen = {}
            for (s_, v_) in waits:
                if v_ > 0 and v_ > seen.get(id(s_), -1):
                    seen[id(s_)] = max(seen.get(id(s_), -1), v_)
            best = {}
            for (s_, v_) in waits:
                if v_ > 0:
                    k_ = id(s_.h)
                    if k_ not in best or v_ > best[k_][1]:
                        best[k_] = (s_.h, v_)
            wl = list(best.values())
            ih = inc.h if inc is not None else None

            def run(e, wl=wl, fn=fn, ih=ih, inck=inck):
                for h, v in wl:
                    e.wait_ge(h, v)
                ins = fn(e)
                if ih is not None:
                    if ins is None:
                        ins = e.nop()
                    ins.then_inc(ih, inck)
            Q[eng].append(run)
            return inc.n if inc is not None else None

        def consume(region, sem):
            REG[region] = [(sem, sem.n)]

        def consume_add(region, sem):
            lst = REG.setdefault(region, [])
            lst[:] = [(s_, v_) for (s_, v_) in lst if s_ is not sem]
            lst.append((sem, sem.n))

        def rwait(*regions):
            out = []
            for r in regions:
                out.extend(REG.get(r, []))
            return out

        def dma(eng, out, in_, sem, waits=()):
            return em(eng, lambda e, o=out, i=in_: e.dma_start(out=o, in_=i),
                      waits=waits, inc=sem, inck=16)

        dma("sync", identf[:], identf_d, sW)
        for t_, d_ in ((cosT, cos_d), (sinT, sin_d), (identb, ident_d), (onesrow, onesr_d),
                       (e2sel, e2_d), (atri, atri_d), (btri, btri_d),
                       (ocol_e, ocole_d), (ocol_o, ocolo_d)):
            dma("sync", t_[:], d_, sW)
        dma("sync", gkv[:, 0:1], gkv_d[0:128, :], sW)
        dma("sync", gkv[:, 1:2], gkv_d[128:256, :], sW)
        for ch in range(32):
            dma("sync", convw[:, 3 * ch:3 * ch + 3], convw_d[128 * ch:128 * ch + 128, :], sW)
            dma("sync", convb[:, ch:ch + 1], convb_d[128 * ch:128 * ch + 128, :], sW)
        w_const = sW.n

        for fn in (lambda e: e.memset(onescol[:], 1.0),
                   lambda e: e.memset(cc[:, 0:1], 1e-6),
                   lambda e: e.memset(cc[:, 1:2], 1e-5),
                   lambda e: e.memset(cc[:, 2:3], LN127),
                   lambda e: e.memset(cc[:, 3:4], 0.0),
                   lambda e: e.memset(cc[:, 4:5], 1e-10),
                   lambda e: e.memset(onesrow_f[:], 1.0)):
            em("vector", fn, inc=sV)
        v_preset = sV.n

        def emit_ring_presets(extra_waits):
            em("vector", lambda e: e.memset(up(0, 128, OFF_KR, 2 * T), 0.0),
               waits=extra_waits + rwait("Kc"), inc=sV)
            em("vector", lambda e: e.memset(up(0, 128, OFF_QRG, 2 * T), 0.0),
               waits=rwait("Qc"), inc=sV)
            em("vector", lambda e: e.memset(up(0, 128, OFF_VR, 2 * 2048), 0.0),
               waits=rwait("Vs"), inc=sV)
            em("vector", lambda e: e.memset(up(64, 1, OFF_KR, T), -1.0), inc=sV)
            em("vector", lambda e: e.memset(up(32, 1, OFF_KR + T, T), -1.0), inc=sV)
            em("vector", lambda e: e.memset(
                UP[0:128, OFF_VR: OFF_VR + 2048].rearrange("p (k c) -> p k c", c=128)[:, :, 64:65],
                1.0), inc=sV)
            em("vector", lambda e: e.memset(
                UP[0:128, OFF_VR + 2048: OFF_VR + 4096].rearrange("p (k c) -> p k c",
                                                                 c=128)[:, :, 0:1],
                1.0), inc=sV)
            for r in ("Kc", "Qc", "Vs"):
                consume_add(r, sV)
            return sV.n

        def w_pieces(wd):
            nr, ncol = wd.shape
            for ch in range(nr // 128):
                for sub in range(ncol // 1024):
                    yield wd[128 * ch:128 * (ch + 1), 1024 * sub:1024 * (sub + 1)]

        def w_scale_pass(wd, nch, prow):
            pieces = list(w_pieces(wd))
            total = len(pieces)
            npc = 0
            for piece in pieces:
                    buf = npc % 2
                    first = (npc == 0)
                    last = (npc == total - 1)
                    npc += 1
                    wsl = wqbuf(buf)
                    ld = dma("sync", wsl, piece, sW,
                             waits=rwait(("wq", buf)))
                    ab = projs[:, 1024 * buf:1024 * (buf + 1)]
                    em("vector", lambda e, wsl=wsl, ab=ab:
                       e.scalar_tensor_tensor(ab, wsl, -1.0, wsl, ALU.mult, ALU.max),
                       waits=[(sW, ld)] + rwait(("wabs", buf), "projs"), inc=sV)
                    consume(("wq", buf), sV)
                    vabs = sV.n
                    for blk in range(2):
                        st = (first and blk == 0)
                        sp = (last and blk == 1)
                        em("tensor", lambda e, ab=ab, blk=blk, st=st, sp=sp, prow=prow:
                           e.matmul(psC[prow:prow + 1, 0:512], onescol[:],
                                    ab[:, 512 * blk:512 * (blk + 1)], start=st, stop=sp),
                           waits=([(sV, vabs), (sV, v_preset)] + (rwait("psC") if st else []))
                           if blk == 0 else [],
                           inc=(sT if blk == 1 else None))
                    consume(("wabs", buf), sT)
            return sT.n

        t_w1s = w_scale_pass(w1t_d, 8, 0)
        t_w2s = w_scale_pass(w2t_d, 32, 32)
        em("vector", lambda e: e.reduce_sum(rows[0:1, 0:1], psC[0:1, 0:512], AXX),
           waits=[(sT, t_w1s)], inc=sV)
        em("vector", lambda e: e.reduce_sum(rows[32:33, 0:1], psC[32:33, 0:512], AXX),
           waits=[(sT, t_w2s)], inc=sV)
        consume("psC", sV)
        em("vector", lambda e: e.tensor_scalar(rows[0:1, 1:2], rows[0:1, 0:1],
                                               1.0 / (D * FH), 1e-5, ALU.mult, ALU.max), inc=sV)
        em("vector", lambda e: e.tensor_scalar(rows[32:33, 1:2], rows[32:33, 0:1],
                                               1.0 / (D * FH), 1e-5, ALU.mult, ALU.max), inc=sV)
        v_means = sV.n
        em("scalar", lambda e: e.activation(rows[0:1, 2:3], rows[0:1, 1:2], AF.Ln),
           waits=[(sV, v_means)], inc=sS)
        em("scalar", lambda e: e.activation(rows[32:33, 2:3], rows[32:33, 1:2], AF.Ln), inc=sS)
        em("scalar", lambda e: e.activation(rows[0:1, 3:4], rows[0:1, 2:3], AF.Exp, scale=-1.0), inc=sS)
        em("scalar", lambda e: e.activation(rows[32:33, 3:4], rows[32:33, 2:3], AF.Exp, scale=-1.0), inc=sS)
        s_winv = sS.n
        em("vector", lambda e: e.tensor_scalar(rows[0:1, 4:5], rows[0:1, 2:3], -LN127, None, ALU.add),
           waits=[(sS, s_winv)], inc=sV)
        v_deqb = sV.n
        specs = ((0, 3, 1, [(sS, s_winv)]), (32, 3, 2, [(sS, s_winv)]),
                 (0, 4, 3, [(sV, v_deqb)]))
        for k, (sp, sc_, dc, wv) in enumerate(specs):
            for half in (0, 64):
                for rep in range(2 if (k == 2 and half == 64) else 1):
                    em("tensor", lambda e, sp=sp, sc_=sc_, k=k, half=half:
                       e.matmul(psC[half:half + 64, k:k + 1], onesrow_f[sp:sp + 1, 0:64],
                                rows[sp:sp + 1, sc_:sc_ + 1], start=True, stop=True),
                       waits=(wv + rwait("psC") + [(sV, v_preset)])
                       if (k == 0 and half == 0) else (wv if half == 0 else []),
                       inc=sT)
        t_b = sT.n
        for rnd in range(2):
            for k, (sp, sc_, dc, wv) in enumerate(specs):
                em("vector", lambda e, dc=dc, k=k:
                   e.tensor_copy(gcol[:, dc:dc + 1], psC[:, k:k + 1]),
                   waits=[(sT, t_b)] if (k == 0 and rnd == 0) else [], inc=sV)
        consume("psC", sV)
        v_wcols = sV.n

        def w_quant_pass(wd, wq_out, nch, swcol):
            npc = 0
            for piece, opiece in zip(w_pieces(wd), w_pieces(wq_out)):
                    buf = npc % 2
                    npc += 1
                    wsl = wqbuf(buf)
                    ld = dma("sync", wsl, piece, sW,
                             waits=rwait(("wq", buf)))
                    cbuf = wqbuf(2)
                    em("vector", lambda e, wsl=wsl, swcol=swcol, cbuf=cbuf:
                       e.tensor_scalar(cbuf, wsl, swcol, 1.25, ALU.mult, ALU.min),
                       waits=[(sW, ld), (sV, v_wcols)] + rwait("stgIq", "stgCD", ("wq", 2)), inc=sV)
                    em("vector", lambda e, cbuf=cbuf:
                       e.tensor_scalar(cbuf, cbuf, -1.25, None, ALU.max),
                       inc=sV)
                    consume(("wq", buf), sV)
                    em("vector", lambda e, cbuf=cbuf:
                       e.tensor_scalar(stgI[:, 0:1024], cbuf, 256.5, None, ALU.add),
                       inc=sV)
                    em("vector", lambda e, cbuf=cbuf:
                       e.scalar_tensor_tensor(stgD[:, 0:1024], stgI[:, 0:1024], -256.5,
                                              cbuf, ALU.add, ALU.is_gt), inc=sV)
                    consume("stgIq", sV)
                    consume(("wq", 2), sV)
                    em("vector", lambda e:
                       e.scalar_tensor_tensor(stgD[:, 0:1024], stgI[:, 0:1024], -256.0,
                                              stgD[:, 0:1024], ALU.add, ALU.subtract), inc=sV)
                    vq = sV.n
                    dma("gpsimd", opiece, stgD[:, 0:1024], sG, waits=[(sV, vq)])
                    consume("stgCD", sG)

        if debug:
            dma("sync", dbg["wrows"][0], rows[0:1, 0:8], sO, waits=[(sV, v_wcols)])
            dma("sync", dbg["wrows"][1], rows[32:33, 0:8], sO, waits=[(sV, v_wcols)])
            dma("sync", dbg["wrows"][2], gcol[0:1, 0:8], sO, waits=[(sV, v_wcols)])
        w_quant_pass(w1t_d, w1q_s, 8, gcol[:, 1:2])
        w_quant_pass(w2t_d, w2q_s, 32, gcol[:, 2:3])
        g_wq_done = sG.n
        if debug:
            dma("sync", dbg["w1q"][:, :], w1q_s[:, :], sO, waits=[(sG, g_wq_done)])
            for ch in range(8):
                dma("sync", dbg["w2q"][128 * ch:128 * (ch + 1), :],
                    w2q_s[128 * ch:128 * (ch + 1), :], sO, waits=[(sG, g_wq_done)])

        def xln_ap(ch, c0, n):
            return up(0, 128, OFF_XLN + ch * T + c0, n)

        def kvn_ap(rch, c0, n):
            return up(0, 128, OFF_KVN + rch * T + c0, n)

        def qc_ap(qch, c0, n):
            return up(0, 128, OFF_QC + qch * T + c0, n)

        def rope_ap(idx, p0, pn, c0, n):
            return up(p0, pn, OFF_ROPE + idx * 4096 + c0, n)

        def kslot(sl, c0, n, p0=0, pn=128):
            return up(p0, pn, OFF_KR + sl * T + c0, n)

        def qslot(sl, c0, n, p0=0, pn=128):
            return up(p0, pn, OFF_QRG + sl * T + c0, n)

        def vslot(sl, c0, n, p0=0, pn=128):
            return up(p0, pn, OFF_VR + sl * 2048 + c0, n)

        def outT_ap(ch, c0, n, p0=0, pn=128):
            return up(p0, pn, OFF_OUTT + ch * T + c0, n)

        def emit_ln_rows(t_sum, t_ssq, bn=512):
            em("vector", lambda e, bn=bn: e.tensor_copy(rows[0:1, 512:512 + bn], psC[0:1, 0:bn]),
               waits=[(sT, t_sum), (sT, t_ssq)], inc=sV)
            em("vector", lambda e, bn=bn: e.tensor_tensor(rows[0:1, 0:bn], rows[0:1, 512:512 + bn],
                                                          rows[0:1, 512:512 + bn], ALU.mult),
               inc=sV)
            em("vector", lambda e, bn=bn:
               e.scalar_tensor_tensor(rows[0:1, 0:bn], rows[0:1, 0:bn], -1.0 / D,
                                      psC[32:33, 0:bn], ALU.mult, ALU.add),
               waits=[(sT, t_ssq)], inc=sV)
            consume("psC", sV)
            em("vector", lambda e, bn=bn:
               e.tensor_scalar(rows[0:1, 0:bn], rows[0:1, 0:bn], 1.0 / D, 1e-5,
                               ALU.mult, ALU.add), inc=sV)
            v_veps = sV.n
            em("scalar", lambda e, bn=bn: e.activation(rows[0:1, 0:bn], rows[0:1, 0:bn], AF.Ln),
               waits=[(sV, v_veps)], inc=sS)
            em("scalar", lambda e, bn=bn: e.activation(rwsB[0:1, 0:bn], rows[0:1, 0:bn],
                                                       AF.Exp, scale=-0.5), inc=sS)
            s_rstd = sS.n
            em("vector", lambda e, bn=bn:
               e.scalar_tensor_tensor(rwsB[0:1, 512:512 + bn], rows[0:1, 512:512 + bn], -1.0 / D,
                                      rwsB[0:1, 0:bn], ALU.mult, ALU.mult),
               waits=[(sS, s_rstd)], inc=sV)
            return s_rstd, sV.n

        def emit_AB_bcast(s_rstd, v_Brow, bn=512):
            em("tensor", lambda e, bn=bn: e.matmul(psA[:, 0:bn], onesrow[:], rwsB[0:1, 0:bn],
                                                   start=True, stop=True),
               waits=[(sS, s_rstd)] + rwait(("psA", 0)), inc=sT)
            em("tensor", lambda e, bn=bn: e.matmul(psA[:, 512:512 + bn], onesrow[:],
                                                   rwsB[0:1, 512:512 + bn], start=True, stop=True),
               waits=[(sV, v_Brow)] + rwait(("psA", 1)), inc=sT)
            return sT.n

        for s in range(NSEQ):
            wkv_sb = WP1[:, 0:2048]
            wq_sb = WP1[:, 2048:6144]
            wkrope_sb = WP1[:, 6144:10240]
            for ch in range(8):
                dma("gpsimd", wkv_sb[:, 256 * ch:256 * (ch + 1)],
                    wkv_d[128 * ch:128 * (ch + 1), :], sG, waits=rwait("WP1"))
            for ch in range(8):
                dma("gpsimd", wq_sb[:, 512 * ch:512 * (ch + 1)],
                    wq_d[128 * ch:128 * (ch + 1), :], sG)
            for ch in range(8):
                dma("gpsimd", wkrope_sb[:, 512 * ch:512 * (ch + 1)],
                    wkrope_d[128 * ch:128 * (ch + 1), :], sG)
            g_p1 = sG.n
            if s == 0:
                wkdec_sb = WP2[:, 0:2048]
                wvdec_sb = WP2[:, 2048:4096]
                wqdec_sb = WP2[:, 4096:8192]
                wqrope_sb = WP2[:, 8192:10240]
                for ch in range(2):
                    dma("gpsimd", wkdec_sb[:, 1024 * ch:1024 * (ch + 1)],
                        wkdec_d[128 * ch:128 * (ch + 1), :], sG)
                    dma("gpsimd", wvdec_sb[:, 1024 * ch:1024 * (ch + 1)],
                        wvdec_d[128 * ch:128 * (ch + 1), :], sG)
                for ch in range(4):
                    dma("gpsimd", wqdec_sb[:, 1024 * ch:1024 * (ch + 1)],
                        wqdec_d[128 * ch:128 * (ch + 1), :], sG)
                for ch in range(4):
                    dma("gpsimd", wqrope_sb[:, 512 * ch:512 * (ch + 1)],
                        wqrope_d[128 * ch:128 * (ch + 1), :], sG)
            g_p2 = sG.n

            for tb in range(NTB):
                ap_in = xin[s].rearrange("(c p) t -> p c t", p=128)[:, :, 512 * tb:512 * (tb + 1)]
                g_x = dma("gpsimd", xbstg[:].rearrange("p (c t) -> p c t", c=8), ap_in, sG,
                          waits=rwait("xbstg"))
                for ch in range(8):
                    em("vector", lambda e, ch=ch:
                       e.tensor_tensor(projs[:, 512 * ch:512 * (ch + 1)],
                                       xbstg[:, 512 * ch:512 * (ch + 1)],
                                       xbstg[:, 512 * ch:512 * (ch + 1)], ALU.mult),
                       waits=([(sG, g_x)] + rwait("projs")) if ch == 0 else [], inc=sV)
                v_sq = sV.n
                for ch in range(8):
                    em("tensor", lambda e, ch=ch:
                       e.matmul(psC[0:1, 0:512], onescol[:], xbstg[:, 512 * ch:512 * (ch + 1)],
                                start=(ch == 0), stop=(ch == 7)),
                       waits=([(sG, g_x), (sV, v_preset)] + rwait("psC")) if ch == 0 else [],
                       inc=(sT if ch == 7 else None))
                t_sum = sT.n
                for ch in range(8):
                    em("tensor", lambda e, ch=ch:
                       e.matmul(psC[32:33, 0:512], onescol[:], projs[:, 512 * ch:512 * (ch + 1)],
                                start=(ch == 0), stop=(ch == 7)),
                       waits=[(sV, v_sq)] if ch == 0 else [],
                       inc=(sT if ch == 7 else None))
                t_ssq = sT.n
                consume_add("projs", sT)
                s_rstd, v_Brow = emit_ln_rows(t_sum, t_ssq)
                t_bc = emit_AB_bcast(s_rstd, v_Brow)
                for ch in range(8):
                    em("vector", lambda e, ch=ch:
                       e.tensor_tensor(stgA[:, 0:512], xbstg[:, 512 * ch:512 * (ch + 1)],
                                       psA[:, 0:512], ALU.mult),
                       waits=([(sT, t_bc)] + (rwait("xlnreg", ("wq", 0), ("wq", 1), ("wq", 2))
                                              if (tb == 0 and ch == 0) else []))
                       if ch == 0 else [], inc=sV)
                    em("vector", lambda e, ch=ch, tb=tb:
                       e.tensor_tensor(xln_ap(ch, 512 * tb, 512), stgA[:, 0:512],
                                       psA[:, 512:1024], ALU.add), inc=sV)
                consume("xbstg", sV)
                consume_add("xbstg", sT)
                consume_add("projs", sV)
                consume(("psA", 0), sV)
                consume(("psA", 1), sV)
            v_xln = sV.n
            if debug:
                for ch in range(8):
                    dma("sync", dbg["x_ln"][s][128 * ch:128 * (ch + 1), :], xln_ap(ch, 0, T),
                        sO, waits=[(sV, v_xln)])

            for tb in range(NTB):
                c0 = 512 * tb
                for rch in range(2):
                    for ch in range(8):
                        em("tensor", lambda e, rch=rch, ch=ch, c0=c0:
                           e.matmul(psB[:, 512 * rch:512 * (rch + 1)],
                                    wkv_sb[:, 256 * ch + 128 * rch: 256 * ch + 128 * (rch + 1)],
                                    xln_ap(ch, c0, 512), start=(ch == 0), stop=(ch == 7)),
                           waits=([(sG, g_p1), (sV, v_xln)] + rwait(("psB", rch))) if ch == 0 else [],
                           inc=(sT if ch == 7 else None))
                t_kv = sT.n
                for rch in range(2):
                    em("vector", lambda e, rch=rch:
                       e.tensor_copy(projs[:, 512 * rch:512 * (rch + 1)],
                                     psB[:, 512 * rch:512 * (rch + 1)]),
                       waits=([(sT, t_kv)] + rwait("ksq", "projs")) if rch == 0 else [], inc=sV)
                    em("vector", lambda e, rch=rch:
                       e.tensor_tensor(ksq[:, 512 * rch:512 * (rch + 1)],
                                       projs[:, 512 * rch:512 * (rch + 1)],
                                       projs[:, 512 * rch:512 * (rch + 1)], ALU.mult), inc=sV)
                v_kvsq = sV.n
                for rch in range(2):
                    em("tensor", lambda e, rch=rch:
                       e.matmul(psC[0:1, 0:512], onescol[:], ksq[:, 512 * rch:512 * (rch + 1)],
                                start=(rch == 0), stop=(rch == 1)),
                       waits=([(sV, v_kvsq)] + rwait("psC")) if rch == 0 else [],
                       inc=(sT if rch == 1 else None))
                t_kvss = sT.n
                consume("ksq", sT)
                em("scalar", lambda e: e.activation(rows[0:1, 0:512], psC[0:1, 0:512], AF.Ln,
                                                    scale=1.0 / KVR, bias=cc[0:1, 0:1]),
                   waits=[(sT, t_kvss)], inc=sS)
                consume("psC", sS)
                em("scalar", lambda e: e.activation(rwsB[0:1, 0:512], rows[0:1, 0:512],
                                                    AF.Exp, scale=-0.5), inc=sS)
                s_rms = sS.n
                em("tensor", lambda e: e.matmul(psA[:, 1024:1536], onesrow[:], rwsB[0:1, 0:512],
                                                start=True, stop=True),
                   waits=[(sS, s_rms)] + rwait(("psA", 2)), inc=sT)
                t_rmsb = sT.n
                for rch in range(2):
                    em("vector", lambda e, rch=rch, c0=c0:
                       e.scalar_tensor_tensor(kvn_ap(rch, c0, 512),
                                              projs[:, 512 * rch:512 * (rch + 1)],
                                              gkv[:, rch:rch + 1], psA[:, 1024:1536],
                                              ALU.mult, ALU.mult),
                       waits=[(sT, t_rmsb)] + (rwait("kvnreg") if (tb == 0 and rch == 0) else []),
                       inc=sV)
                consume(("psB", 0), sV)
                consume(("psB", 1), sV)
                consume(("psA", 2), sV)
                consume_add("projs", sV)
                for qch in range(4):
                    blk = 2 + (qch % 2)
                    for ch in range(8):
                        em("tensor", lambda e, qch=qch, ch=ch, c0=c0, blk=blk:
                           e.matmul(psB[:, 512 * blk:512 * (blk + 1)],
                                    wq_sb[:, 512 * ch + 128 * qch: 512 * ch + 128 * (qch + 1)],
                                    xln_ap(ch, c0, 512), start=(ch == 0), stop=(ch == 7)),
                           waits=(rwait(("psB", blk))) if ch == 0 else [],
                           inc=(sT if ch == 7 else None))
                    t_q = sT.n
                    em("vector", lambda e, qch=qch, c0=c0, blk=blk:
                       e.tensor_scalar(qc_ap(qch, c0, 512), psB[:, 512 * blk:512 * (blk + 1)],
                                       ISQ96, None, ALU.mult),
                       waits=[(sT, t_q)] + (rwait("qcreg") if (tb == 0 and qch == 0) else []),
                       inc=sV)
                    consume(("psB", blk), sV)
                v_qc = sV.n

                def rope_block(widx, wsb, nkc, src_ap_fn, extra_w, c0=c0, tb=tb):
                    for half in range(2):
                        for ab in range(2):
                            kch = 2 * ab + half
                            for ch in range(nkc):
                                em("tensor", lambda e, kch=kch, ch=ch, ab=ab, wsb=wsb,
                                   src_ap_fn=src_ap_fn, nkc=nkc:
                                   e.matmul(psA[:, 512 * ab:512 * (ab + 1)],
                                            wsb[:, 512 * ch + 128 * kch: 512 * ch + 128 * (kch + 1)],
                                            src_ap_fn(ch), start=(ch == 0), stop=(ch == nkc - 1)),
                                   waits=(extra_w + rwait(("psA", ab))) if ch == 0 else [],
                                   inc=(sT if ch == nkc - 1 else None))
                        t_r = sT.n
                        cs = cosT[:, c0:c0 + 512]
                        sn = sinT[:, c0:c0 + 512]
                        em("vector", lambda e, cs=cs: e.tensor_tensor(stgA[:, 0:512], psA[:, 0:512],
                                                                      cs, ALU.mult),
                           waits=[(sT, t_r)] + (rwait("ropereg") if (tb == 0 and half == 0) else []),
                           inc=sV)
                        em("vector", lambda e, sn=sn: e.tensor_tensor(stgB[:, 0:512],
                                                                      psA[:, 512:1024], sn,
                                                                      ALU.mult), inc=sV)
                        em("vector", lambda e, half=half, widx=widx, c0=c0:
                           e.tensor_tensor(rope_ap(widx, 0, 128, half * T + c0, 512),
                                           stgA[:, 0:512], stgB[:, 0:512], ALU.subtract), inc=sV)
                        em("vector", lambda e, cs=cs: e.tensor_tensor(stgA[:, 0:512],
                                                                      psA[:, 512:1024], cs,
                                                                      ALU.mult), inc=sV)
                        em("vector", lambda e, sn=sn: e.tensor_tensor(stgB[:, 0:512], psA[:, 0:512],
                                                                      sn, ALU.mult), inc=sV)
                        em("vector", lambda e, half=half, widx=widx, c0=c0:
                           e.tensor_tensor(rope_ap(widx + 1, 0, 128, half * T + c0, 512),
                                           stgA[:, 0:512], stgB[:, 0:512], ALU.add), inc=sV)
                        consume(("psA", 0), sV)
                        consume(("psA", 1), sV)

                rope_block(0, wkrope_sb, 8, lambda ch, c0=c0: xln_ap(ch, c0, 512), [])
                rope_block(2, wqrope_sb, 4, lambda ch, c0=c0: qc_ap(ch, c0, 512),
                           [(sV, v_qc), (sG, g_p2)])
            v_proj = sV.n
            t_proj = sT.n
            consume_add("WP1", sT)
            if debug:
                for rch in range(2):
                    dma("sync", dbg["kvn"][s][128 * rch:128 * (rch + 1), :], kvn_ap(rch, 0, T),
                        sO, waits=[(sV, v_proj)])
                for qch in range(4):
                    dma("sync", dbg["qc"][s][128 * qch:128 * (qch + 1), :], qc_ap(qch, 0, T),
                        sO, waits=[(sV, v_proj)])
                for i, nm in ((0, "ropek"), (2, "ropeq")):
                    for j2 in range(2):
                        dma("sync", dbg[nm][s][j2], rope_ap(i + j2, 0, 128, 0, 2 * T),
                            sO, waits=[(sV, v_proj)])

            def prep_pair(pair):
                h0 = 2 * pair
                for tb in range(4):
                    c0 = 512 * tb
                    for rch in range(2):
                        em("tensor", lambda e, rch=rch, c0=c0, pair=pair:
                           e.matmul(psA[:, 0:512],
                                    wkdec_sb[:, 1024 * rch + 128 * pair: 1024 * rch + 128 * (pair + 1)],
                                    kvn_ap(rch, c0, 512), start=(rch == 0), stop=(rch == 1)),
                           waits=([(sV, v_proj)] + rwait(("psA", 0))) if rch == 0 else [],
                           inc=(sT if rch == 1 else None))
                    t_k = sT.n
                    em("vector", lambda e, c0=c0: e.tensor_copy(kslot(0, c0, 512, 0, 64),
                                                                psA[0:64, 0:512]),
                       waits=[(sT, t_k)] + rwait("Kc"), inc=sV)
                    em("vector", lambda e, c0=c0: e.tensor_copy(kslot(1, c0, 512, 64, 64),
                                                                psA[64:128, 0:512]), inc=sV)
                    consume(("psA", 0), sV)
                for tb in range(4):
                    c0 = 512 * tb
                    for qch in range(4):
                        em("tensor", lambda e, qch=qch, c0=c0, pair=pair:
                           e.matmul(psA[:, 512:1024],
                                    wqdec_sb[:, 1024 * qch + 128 * pair: 1024 * qch + 128 * (pair + 1)],
                                    qc_ap(qch, c0, 512), start=(qch == 0), stop=(qch == 3)),
                           waits=(rwait(("psA", 1))) if qch == 0 else [],
                           inc=(sT if qch == 3 else None))
                    t_q = sT.n
                    em("vector", lambda e, c0=c0: e.tensor_copy(qslot(0, c0, 512, 0, 64),
                                                                psA[0:64, 512:1024]),
                       waits=[(sT, t_q)] + rwait("Qc"), inc=sV)
                    em("vector", lambda e, c0=c0: e.tensor_copy(qslot(1, c0, 512, 64, 64),
                                                                psA[64:128, 512:1024]), inc=sV)
                    consume(("psA", 1), sV)
                v_cont = sV.n
                for sl in range(2):
                    hh = h0 + sl
                    for tc in range(16):
                        for rch in range(2):
                            em("tensor", lambda e, tc=tc, rch=rch, hh=hh:
                               e.matmul(psA[:, 64 * tc:64 * (tc + 1)],
                                        kvn_ap(rch, 128 * tc, 128),
                                        wvdec_sb[:, 1024 * rch + 64 * hh: 1024 * rch + 64 * (hh + 1)],
                                        start=(rch == 0), stop=(rch == 1)),
                               waits=(rwait(("psA", 0), ("psA", 1))) if (tc == 0 and rch == 0) else [],
                               inc=(sT if (rch == 1 and tc == 15) else None))
                    t_v = sT.n
                    voff = 64 if sl == 0 else 64
                    dcol = 0 if sl == 0 else 64
                    em("vector", lambda e, sl=sl, dcol=dcol:
                       e.tensor_copy(
                           UP[0:128, OFF_VR + sl * 2048: OFF_VR + (sl + 1) * 2048]
                           .rearrange("p (k c) -> p k c", c=128)[:, :, dcol:dcol + 64],
                           psA[:, 0:1024].rearrange("p (k c) -> p k c", c=64)),
                       waits=[(sT, t_v)] + rwait("Vs"), inc=sV)
                    consume(("psA", 0), sV)
                    consume(("psA", 1), sV)
                v_vdec = sV.n
                pa = 16 * (h0 % 8)
                half = h0 // 8
                for idx, base_off in ((0, OFF_KR), (1, OFF_KR), (2, OFF_QRG), (3, OFF_QRG)):
                    rr_e = 65 if idx % 2 == 0 else 81
                    rr_o = 0 if idx % 2 == 0 else 16
                    dma("gpsimd", UP[rr_e:rr_e + 16, base_off: base_off + T],
                        rope_ap(idx, pa, 16, half * T, T), sG,
                        waits=[(sV, v_proj)] + rwait("Kc" if base_off == OFF_KR else "Qc"))
                    dma("gpsimd", UP[rr_o:rr_o + 16, base_off + T: base_off + 2 * T],
                        rope_ap(idx, pa + 16, 16, half * T, T), sG)
                g_rope = sG.n
                return v_cont, v_vdec, g_rope

            def bG_chain(sl, hh, v_ready, g_ready):
                par = hh % 2
                prow = 64 if par == 0 else 32
                ocol = ocol_e if par == 0 else ocol_o
                em("vector", lambda e, sl=sl: e.tensor_tensor(ksq[:], kslot(sl, 0, T),
                                                              kslot(sl, 0, T), ALU.mult),
                   waits=[(sV, v_ready), (sG, g_ready), (sV, v_preset)] + rwait("ksq"), inc=sV)
                v_ksq = sV.n
                consume_add("Kc", sV)
                for blk in range(4):
                    em("tensor", lambda e, blk=blk, prow=prow:
                       e.matmul(psC[prow:prow + 1, 0:512], onescol[:],
                                ksq[:, 512 * blk:512 * (blk + 1)], start=True, stop=True),
                       waits=[(sV, v_ksq)] + rwait("psC"), inc=sT)
                    t_g = sT.n
                    em("vector", lambda e, blk=blk, prow=prow:
                       e.reduce_max(gcol[prow:prow + 1, 4 + blk:5 + blk],
                                    psC[prow:prow + 1, 0:512], AXX),
                       waits=[(sT, t_g)], inc=sV)
                    consume("psC", sV)
                em("vector", lambda e, prow=prow:
                   e.reduce_max(gcol[prow:prow + 1, 0:1], gcol[prow:prow + 1, 4:8], AXX), inc=sV)
                v_G = sV.n
                em("vector", lambda e, sl=sl: e.tensor_tensor(ksq[:], qslot(sl, 0, T),
                                                              qslot(sl, 0, T), ALU.mult),
                   waits=[(sV, v_ready), (sG, g_ready)] + rwait("ksq"), inc=sV)
                v_qsq = sV.n
                consume_add("Qc", sV)
                for qb in range(4):
                    em("tensor", lambda e, qb=qb, prow=prow, ocol=ocol:
                       e.matmul(psB[prow:prow + 1, 512 * qb:512 * (qb + 1)], ocol[:],
                                ksq[:, 512 * qb:512 * (qb + 1)], start=True, stop=True),
                       waits=([(sV, v_qsq)] + rwait("psBav", "psBrow")) if qb == 0 else [], inc=sT)
                t_b = sT.n
                consume("ksq", sT)
                em("scalar", lambda e, prow=prow:
                   e.activation(ksq[prow:prow + 1, 0:T], psB[prow:prow + 1, 0:T], AF.Ln,
                                scale=gcol[prow:prow + 1, 0:1], bias=cc[prow:prow + 1, 4:5]),
                   waits=[(sT, t_b), (sV, v_G)], inc=sS)
                em("scalar", lambda e, sl=sl, prow=prow:
                   e.activation(qslot(sl, 0, T, prow, 1), ksq[prow:prow + 1, 0:T],
                                AF.Exp, scale=0.5), inc=sS)
                consume("psBrow", sS)
                consume_add("ksq", sS)
                return sS.n

            def inner_head(sl, hh, s_b, kt_ctr):
                par = hh % 2
                lrow = 64 if par == 0 else 0
                drow = 0 if par == 0 else 64
                av_last = 0
                first_av = True
                for qb in range(4):
                    q0 = 512 * qb
                    for kt in range(4 * qb + 4):
                        ring = kt_ctr[0] % 3
                        kt_ctr[0] += 1
                        qs = max(q0, 128 * kt)
                        n = 512 * (qb + 1) - qs
                        diag = (128 * kt >= q0)
                        em("tensor", lambda e, sl=sl, kt=kt, qs=qs, n=n, ring=ring, diag=diag:
                           e.matmul(psA[:, 512 * ring: 512 * ring + n],
                                    kslot(sl, 128 * kt, 128), qslot(sl, qs, n),
                                    start=True, stop=not diag),
                           waits=[(sS, s_b)] + rwait(("psA", ring)),
                           inc=(None if diag else sT))
                        if diag:
                            em("tensor", lambda e, ring=ring:
                               e.matmul(psA[:, 512 * ring: 512 * ring + 128],
                                        atri[:], btri[:, 0:128], start=False, stop=True),
                               inc=sT)
                        t_sc = sT.n
                        em("scalar", lambda e, ring=ring, n=n:
                           e.activation(attnT[:, 512 * ring: 512 * ring + n],
                                        psA[:, 512 * ring: 512 * ring + n], AF.Exp),
                           waits=[(sT, t_sc)] + rwait(("attnT", ring)), inc=sS)
                        consume(("psA", ring), sS)
                        s_exp = sS.n
                        em("tensor", lambda e, sl=sl, kt=kt, qs=qs, n=n, ring=ring,
                           last=(kt == 4 * qb + 3):
                           e.matmul(psB[:, qs:qs + n], vslot(sl, 128 * kt, 128),
                                    attnT[:, 512 * ring: 512 * ring + n],
                                    start=(kt == 0), stop=last),
                           waits=[(sS, s_exp)] + (rwait("psBav") if first_av else []),
                           inc=sT)
                        first_av = False
                        consume(("attnT", ring), sT)
                        av_last = sT.n
                for lh in range(2):
                    em("scalar", lambda e, lrow=lrow, lh=lh:
                       e.activation(rows[lrow:lrow + 1, 0:1024],
                                    psB[lrow:lrow + 1, 1024 * lh:1024 * (lh + 1)],
                                    AF.Copy, bias=0.0),
                       waits=[(sT, av_last)] + rwait("lrow"), inc=sS)
                    s_l = sS.n
                    dma("gpsimd", Lbuf[hh:hh + 1, 1024 * lh:1024 * (lh + 1)],
                        rows[lrow:lrow + 1, 0:1024], sG,
                        waits=[(sS, s_l)] + rwait("Lbufreg"))
                    consume("lrow", sG)
                ch = hh // 2
                em("vector", lambda e, ch=ch, drow=drow:
                   e.tensor_copy(outT_ap(ch, 0, T, drow, 64), psB[drow:drow + 64, 0:T]),
                   waits=[(sT, av_last)], inc=sV)
                consume("psBav", sV)
                consume_add("psBav", sG)

            emit_ring_presets([(sW, sW.n)])
            kt_ctr = [0]
            for pair in range(8):
                v_cont, v_vdec, g_rope = prep_pair(pair)
                sb0 = bG_chain(0, 2 * pair, v_cont, g_rope)
                sb1 = bG_chain(1, 2 * pair + 1, v_cont, g_rope)
                if debug and s == 0 and pair < 2:
                    for sl in range(2):
                        dma("sync", dbg["kring"][2 * pair + sl], kslot(sl, 0, T), sO,
                            waits=[(sS, sb1), (sV, v_vdec)])
                        dma("sync", dbg["qring"][2 * pair + sl], qslot(sl, 0, T), sO,
                            waits=[(sS, sb1), (sV, v_vdec)])
                    if pair == 0:
                        for sl in range(2):
                            dma("sync", dbg["vring"][sl], vslot(sl, 0, 2048), sO,
                                waits=[(sV, v_vdec)])
                inner_head(0, 2 * pair, sb0, kt_ctr)
                inner_head(1, 2 * pair + 1, sb1, kt_ctr)
                consume_add("Kc", sT)
                consume_add("Qc", sT)
                consume_add("Vs", sT)
            consume("kvnreg", sT)
            consume("qcreg", sT)
            consume("ropereg", sG)

            v_lcl = em("vector", lambda e: e.tensor_scalar(Lbuf[:], Lbuf[:], 1e-19, None, ALU.max),
                       waits=[(sG, sG.n)], inc=sV)
            em("scalar", lambda e: e.activation(Lbuf[:], Lbuf[:], AF.Ln),
               waits=[(sV, v_lcl)], inc=sS)
            em("scalar", lambda e: e.activation(LbufR[:], Lbuf[:], AF.Exp, scale=-1.0),
               waits=rwait("LbufRreg"), inc=sS)
            s_lr = sS.n
            consume("Lbufreg", sS)
            if debug:
                dma("sync", dbg["lrows"][s], LbufR[:], sO, waits=[(sS, s_lr)])
            for ch in range(8):
                g_rl = dma("gpsimd", rl2[:], LbufR[2 * ch:2 * ch + 2, :], sG,
                           waits=[(sS, s_lr)] + rwait("rl2", "ksq"))
                for tb in range(4):
                    em("tensor", lambda e, tb=tb:
                       e.matmul(psC[:, 0:512], e2sel[:], rl2[:, 512 * tb:512 * (tb + 1)],
                                start=True, stop=True),
                       waits=[(sG, g_rl)] + rwait("psC"), inc=sT)
                    t_n = sT.n
                    em("vector", lambda e, ch=ch, tb=tb:
                       e.tensor_tensor(outT_ap(ch, 512 * tb, 512), outT_ap(ch, 512 * tb, 512),
                                       psC[:, 0:512], ALU.mult),
                       waits=[(sT, t_n)], inc=sV)
                    consume("psC", sV)
                consume("rl2", sV)
                consume_add("ksq", sG)
            v_norm = sV.n
            consume("LbufRreg", sG)
            if debug:
                for ch in range(8):
                    dma("sync", dbg["outT"][s][128 * ch:128 * (ch + 1), :], outT_ap(ch, 0, T),
                        sO, waits=[(sV, v_norm)])
            wo_sb = WP1[:, 0:8192]
            for ch in range(8):
                dma("gpsimd", wo_sb[:, 1024 * ch:1024 * (ch + 1)],
                    wo_d[128 * ch:128 * (ch + 1), :], sG, waits=rwait("WP1"))
            g_wo = sG.n
            for tb in range(4):
                c0 = 512 * tb
                for jc in range(8):
                    blk = jc % 4
                    for ch in range(8):
                        em("tensor", lambda e, jc=jc, ch=ch, c0=c0, blk=blk:
                           e.matmul(psB[:, 512 * blk:512 * (blk + 1)],
                                    wo_sb[:, 1024 * ch + 128 * jc: 1024 * ch + 128 * (jc + 1)],
                                    outT_ap(ch, c0, 512), start=(ch == 0), stop=(ch == 7)),
                           waits=([(sG, g_wo), (sV, v_norm)] + rwait(("psB", blk), "psBav"))
                           if ch == 0 else [],
                           inc=(sT if ch == 7 else None))
                    t_p = sT.n
                    em("vector", lambda e, jc=jc, blk=blk:
                       e.tensor_copy(projs[:, 512 * jc:512 * (jc + 1)],
                                     psB[:, 512 * blk:512 * (blk + 1)]),
                       waits=[(sT, t_p)] + (rwait("projs") if jc == 0 else []), inc=sV)
                    em("vector", lambda e, jc=jc:
                       e.tensor_tensor(ffout[:, 512 * jc:512 * (jc + 1)],
                                       projs[:, 512 * jc:512 * (jc + 1)],
                                       projs[:, 512 * jc:512 * (jc + 1)], ALU.mult),
                       waits=(rwait("ffout") if jc == 0 else []), inc=sV)
                    consume(("psB", blk), sV)
                v_pe = sV.n
                for ch in range(8):
                    em("tensor", lambda e, ch=ch:
                       e.matmul(psC[0:1, 0:512], onescol[:], projs[:, 512 * ch:512 * (ch + 1)],
                                start=(ch == 0), stop=(ch == 7)),
                       waits=([(sV, v_pe)] + rwait("psC")) if ch == 0 else [],
                       inc=(sT if ch == 7 else None))
                t_sum = sT.n
                for ch in range(8):
                    em("tensor", lambda e, ch=ch:
                       e.matmul(psC[32:33, 0:512], onescol[:], ffout[:, 512 * ch:512 * (ch + 1)],
                                start=(ch == 0), stop=(ch == 7)),
                       waits=[(sV, v_pe)] if ch == 0 else [], inc=(sT if ch == 7 else None))
                t_ssq = sT.n
                consume_add("projs", sT)
                consume_add("ffout", sT)
                s_rstd, v_Brow = emit_ln_rows(t_sum, t_ssq)
                t_bc = emit_AB_bcast(s_rstd, v_Brow)
                ap_in = xin[s].rearrange("(c p) t -> p c t", p=128)[:, :, c0:c0 + 512]
                g_x = dma("gpsimd", xbstg[:].rearrange("p (c t) -> p c t", c=8), ap_in, sG,
                          waits=rwait("xbstg"))
                for ch in range(8):
                    em("vector", lambda e, ch=ch:
                       e.tensor_tensor(stgA[:, 0:512], projs[:, 512 * ch:512 * (ch + 1)],
                                       psA[:, 0:512], ALU.mult),
                       waits=[(sT, t_bc), (sG, g_x)] if ch == 0 else [], inc=sV)
                    em("vector", lambda e:
                       e.tensor_tensor(stgB[:, 0:512], stgA[:, 0:512], psA[:, 512:1024], ALU.add),
                       inc=sV)
                    em("vector", lambda e, ch=ch:
                       e.tensor_tensor(outstg[:, 0:512], stgB[:, 0:512],
                                       xbstg[:, 512 * ch:512 * (ch + 1)], ALU.add),
                       waits=rwait("outstg"), inc=sV)
                    v_x1 = sV.n
                    dma("sync", x1d[s][128 * ch:128 * (ch + 1), c0:c0 + 512],
                        outstg[:, 0:512], sW, waits=[(sV, v_x1)])
                    consume("outstg", sW)
                consume("xbstg", sV)
                consume_add("projs", sV)
                consume(("psA", 0), sV)
                consume(("psA", 1), sV)
            consume("WP1", sT)
            consume("xlnreg", sT)
            x1_wr = sW.n
            if debug:
                for ch in range(8):
                    dma("sync", dbg["x1"][s][128 * ch:128 * (ch + 1), :],
                        x1d[s][128 * ch:128 * (ch + 1), :], sO, waits=[(sW, x1_wr)])

        def ffo_ap(jc, c0, n):
            if jc < 4:
                return ffout[:, 1024 * jc + c0: 1024 * jc + c0 + n]
            if jc < 6:
                return cosT[:, 1024 * (jc - 4) + c0: 1024 * (jc - 4) + c0 + n]
            return sinT[:, 1024 * (jc - 6) + c0: 1024 * (jc - 6) + c0 + n]

        def emit_qscale(nsub, subs, deq, v_tree, dbg_wait=0, dbg_tap=False, base=0):
            RAW0, LN0, SC0, DQ0 = base, base + 10, base + 20, base + 30
            n = len(subs)

            def fq(reg):
                return psC[:, 0:128] if reg == 0 else psB[:, 1536:1664]
            for i in range(n):
                sc_, sn_ = subs[i]
                for rep in range(2 if i == n - 1 else 1):
                    em("tensor", lambda e, sc_=sc_, sn_=sn_, reg=i % 2:
                       e.matmul(fq(reg)[0:sn_, 0:128], stgD[:, sc_:sc_ + sn_], identb[:],
                                start=True, stop=True),
                       waits=[(sV, v_tree)] + rwait(("psCq", i % 2))
                       + (rwait(("psB", 3), "psBav") if i == 1 else [])
                       + ([(sO, dbg_wait)] if (dbg_wait and i == 0) else []),
                       inc=sT)
                t_tr = sT.n
                em("vector", lambda e, i=i, reg=i % 2:
                   e.reduce_max(mxcol[:, RAW0 + i:RAW0 + i + 1],
                                fq(reg)[0:128, 0:128], AXX),
                   waits=[(sT, t_tr)], inc=sV)
                consume(("psCq", i % 2), sV)
            em("vector", lambda e, nsub=nsub:
               e.tensor_scalar(mxcol[:, RAW0:RAW0 + nsub], mxcol[:, RAW0:RAW0 + nsub],
                               1e-10, None, ALU.max), inc=sV)
            if dbg_tap:
                lreg = (n - 1) % 2
                dma("sync", dbg["stgd2"], stgD[:, 0:1026], sO, waits=[(sT, t_tr)])
                em("vector", lambda e, lreg=lreg:
                   e.tensor_copy(stgA[:, 0:128], fq(lreg)[0:128, 0:128]), inc=sV)
                vps = sV.n
                dma("sync", dbg["psdump"], stgA[:, 0:128], sO, waits=[(sV, vps)])
                em("vector", lambda e, lreg=lreg:
                   e.reduce_max(mxcol[:, 89:90], fq(lreg)[0:128, 0:128], AXX), inc=sV)
            v_mx = sV.n
            em("scalar", lambda e, nsub=nsub:
               e.activation(mxcol[:, LN0:LN0 + nsub], mxcol[:, RAW0:RAW0 + nsub], AF.Ln),
               waits=[(sV, v_mx)], inc=sS)
            em("scalar", lambda e, nsub=nsub:
               e.activation(mxcol[:, SC0:SC0 + nsub], mxcol[:, LN0:LN0 + nsub], AF.Exp,
                            scale=-0.5, bias=cc[:, 2:3]), inc=sS)
            if deq:
                em("scalar", lambda e, nsub=nsub:
                   e.activation(mxcol[:, DQ0:DQ0 + nsub], mxcol[:, LN0:LN0 + nsub], AF.Exp,
                                scale=0.5, bias=gcol[:, 3:4]), inc=sS)
            for _ in range(2):
                em("vector", lambda e, c=RAW0 + n - 1, reg=(n - 1) % 2:
                   e.reduce_max(mxcol[:, c:c + 1], fq(reg)[0:128, 0:128], AXX),
                   inc=sV)
            consume(("psCq", (n - 1) % 2), sV)
            em("vector", lambda e, c=RAW0 + n - 1:
               e.tensor_scalar(mxcol[:, c:c + 1], mxcol[:, c:c + 1],
                               1e-10, None, ALU.max), inc=sV)
            v_mx2 = sV.n
            em("scalar", lambda e, c=n - 1:
               e.activation(mxcol[:, LN0 + c:LN0 + c + 1], mxcol[:, RAW0 + c:RAW0 + c + 1],
                            AF.Ln), waits=[(sV, v_mx2)], inc=sS)
            em("scalar", lambda e, c=n - 1:
               e.activation(mxcol[:, SC0 + c:SC0 + c + 1], mxcol[:, LN0 + c:LN0 + c + 1],
                            AF.Exp, scale=-0.5, bias=cc[:, 2:3]), inc=sS)
            if deq:
                em("scalar", lambda e, c=n - 1:
                   e.activation(mxcol[:, DQ0 + c:DQ0 + c + 1], mxcol[:, LN0 + c:LN0 + c + 1],
                                AF.Exp, scale=0.5, bias=gcol[:, 3:4]), inc=sS)
            s_sc = sS.n
            for r in range(2 if deq else 1):
                cbase = SC0 if r == 0 else DQ0
                rc0 = 512 * r
                def br(reg):
                    return psC[0:1, 0:128] if reg == 0 else psB[0:1, 1536:1664]
                for j in range(n):
                    sc_, sn_ = subs[j]
                    for rep in range(2 if j == n - 1 else 1):
                        em("tensor", lambda e, cbase=cbase, j=j, reg=j % 2:
                           e.matmul(br(reg), mxcol[:, cbase + j:cbase + j + 1], identf[:],
                                    start=True, stop=True),
                           waits=[(sS, s_sc)] + rwait(("psCr", j % 2), ("psCq", j % 2))
                           + (rwait(("psB", 3), "psBav") if (j == 1 and r == 0) else []),
                           inc=sT)
                    t_tb = sT.n
                    em("vector", lambda e, rc0=rc0, sn_=sn_, reg=j % 2:
                       e.tensor_copy(rwsB[0:1, rc0:rc0 + sn_], br(reg)[0:1, 0:sn_]),
                       waits=[(sT, t_tb)], inc=sV)
                    if j == n - 1:
                        em("vector", lambda e, rc0=rc0, sn_=sn_, reg=j % 2:
                           e.tensor_copy(rwsB[0:1, rc0:rc0 + sn_], br(reg)[0:1, 0:sn_]),
                           inc=sV)
                    consume(("psCr", j % 2), sV)
                    v_row = sV.n
                    tgt = psB if r == 0 else psA
                    if sc_ < 1024 < sc_ + sn_:
                        pieces = [(sc_, 0, 1024 - sc_), (1024, 1024 - sc_, sc_ + sn_ - 1024),
                                  (sc_, 0, 1024 - sc_)]
                    else:
                        pieces = [(sc_, 0, sn_)] * (2 if j == n - 1 else 1)
                    for pi, (pc, po, pw) in enumerate(pieces):
                        em("tensor", lambda e, pc=pc, po=po, pw=pw, tgt=tgt, rc0=rc0:
                           e.matmul(tgt[:, pc:pc + pw], onesrow[0:1, 0:128],
                                    rwsB[0:1, rc0 + po:rc0 + po + pw], start=True, stop=True),
                           waits=[(sV, v_row)] + ((rwait(("psB", 0), ("psB", 1), ("psB", 2),
                                                         ("psB", 3), "psBav") if r == 0 else
                                                   rwait(("psA", 0), ("psA", 1), ("psA", 2)))
                                                  if (j == 0 and pi == 0) else []),
                           inc=sT)

            t_ret = sT.n
            if dbg_tap:
                dma("sync", dbg["mxh"], mxcol[:], sO, waits=[(sS, s_sc)])
                em("vector", lambda e: e.tensor_copy(ffout[:, 0:1024], psB[:, 0:1024]),
                   waits=[(sT, t_ret)], inc=sV)
                vv = sV.n
                dd = dma("sync", dbg["sbch"], ffout[:, 0:1024], sO, waits=[(sV, vv)])
                em("tensor", lambda e: None, waits=[(sO, dd)])
            return t_ret

        def hbuf_ap(ch, c0, n):
            return up(0, 128, OFF_HBUF + ch * 1026 + c0, n)

        def x1b_ap(ch, c0, n):
            return up(0, 128, OFF_X1B + ch * 1026 + c0, n)

        def x2ln_ap(ch, c0, n):
            return up(0, 128, OFF_X2LN + ch * 1026 + c0, n)

        def xq_ap(ch, c0, n):
            return up(0, 128, OFF_XQ + ch * 1026 + c0, n)

        LN2BLOCKS = [(0, 512), (512, 512), (514, 512)]
        W1BLOCKS = [(1024, 2), (0, 512), (512, 512)]
        x1_final_wr = sW.n

        for s in range(NSEQ):
            for ft in range(NFT):
                t0 = ft * FTILE
                lo = t0 - 1
                n_lo = 1 if t0 == 0 else 0
                hi = t0 + FTILE + 1
                n_hi = 1 if hi > T else 0
                src_lo = max(lo, 0)
                src_n = (FTILE + 2) - n_lo - n_hi
                g_x1 = dma("gpsimd",
                           UP[0:128, OFF_X1B: OFF_X1B + 8 * 1026].rearrange(
                               "p (c t) -> p c t", c=8)[:, :, n_lo:n_lo + src_n],
                           x1d[s].rearrange("(c p) t -> p c t", p=128)[:, :, src_lo:src_lo + src_n],
                           sG, waits=[(sW, x1_final_wr if (s == 0 and ft == 0) else x1_final_wr)]
                           + rwait("x1b"))
                v_halo = sV.n
                if n_lo:
                    v_halo = em("vector", lambda e: e.memset(
                        UP[0:128, OFF_X1B: OFF_X1B + 8 * 1026].rearrange(
                            "p (c t) -> p c t", c=8)[:, :, 0:1], 0.0),
                        waits=rwait("x1b"), inc=sV)
                if n_hi:
                    v_halo = em("vector", lambda e: e.memset(
                        UP[0:128, OFF_X1B: OFF_X1B + 8 * 1026].rearrange(
                            "p (c t) -> p c t", c=8)[:, :, 1025:1026], 0.0),
                        waits=rwait("x1b"), inc=sV)

                for (b0, bn) in LN2BLOCKS:
                    for ch in range(8):
                        em("vector", lambda e, ch=ch, b0=b0, bn=bn:
                           e.tensor_tensor(projs[:, 512 * ch:512 * ch + bn], x1b_ap(ch, b0, bn),
                                           x1b_ap(ch, b0, bn), ALU.mult),
                           waits=([(sG, g_x1), (sV, v_halo)] + rwait("projs")) if ch == 0 else [],
                           inc=sV)
                    v_s = sV.n
                    for ch in range(8):
                        em("tensor", lambda e, ch=ch, b0=b0, bn=bn:
                           e.matmul(psC[0:1, 0:bn], onescol[:], x1b_ap(ch, b0, bn),
                                    start=(ch == 0), stop=(ch == 7)),
                           waits=([(sG, g_x1), (sV, v_halo)] + rwait("psC")) if ch == 0 else [],
                           inc=(sT if ch == 7 else None))
                    t_sum = sT.n
                    for ch in range(8):
                        em("tensor", lambda e, ch=ch, bn=bn:
                           e.matmul(psC[32:33, 0:bn], onescol[:], projs[:, 512 * ch:512 * ch + bn],
                                    start=(ch == 0), stop=(ch == 7)),
                           waits=[(sV, v_s)] if ch == 0 else [],
                           inc=(sT if ch == 7 else None))
                    t_ssq = sT.n
                    consume_add("projs", sT)
                    s_rstd, v_Brow = emit_ln_rows(t_sum, t_ssq, bn)
                    t_bc = emit_AB_bcast(s_rstd, v_Brow, bn)
                    for ch in range(8):
                        em("vector", lambda e, ch=ch, b0=b0, bn=bn:
                           e.tensor_tensor(stgA[:, 0:bn], x1b_ap(ch, b0, bn), psA[:, 0:bn],
                                           ALU.mult),
                           waits=[(sT, t_bc)] if ch == 0 else [], inc=sV)
                        em("vector", lambda e, ch=ch, b0=b0, bn=bn:
                           e.tensor_tensor(x2ln_ap(ch, b0, bn), stgA[:, 0:bn],
                                           psA[:, 512:512 + bn], ALU.add), inc=sV)
                    consume(("psA", 0), sV)
                    consume(("psA", 1), sV)
                v_x2 = sV.n

                for ch in range(8):
                    em("vector", lambda e, ch=ch: e.tensor_tensor(
                        stgC[:, 0:1026], x2ln_ap(ch, 0, 1026), x2ln_ap(ch, 0, 1026), ALU.mult),
                        waits=([(sV, v_x2)] + rwait("stgCD")) if ch == 0 else [], inc=sV)
                    if ch == 0:
                        em("vector", lambda e: e.tensor_copy(stgD[:, 0:1026], stgC[:, 0:1026]),
                           inc=sV)
                    else:
                        em("vector", lambda e: e.tensor_tensor(stgD[:, 0:1026], stgD[:, 0:1026],
                                                               stgC[:, 0:1026], ALU.max), inc=sV)
                v_tree = sV.n
                subs = [(i * 128, 128) for i in range(8)] + [(898, 128)]
                t_bcq = emit_qscale(9, subs, True, v_tree)
                em("vector", lambda e:
                   e.tensor_copy(stgC[:, 0:1026], psB[:, 0:1026]),
                   waits=[(sT, t_bcq)] + rwait("stgCD"), inc=sV)
                em("vector", lambda e:
                   e.tensor_copy(stgC[:, 0:1026], psA[:, 0:1026]), inc=sV)
                for ch in range(8):
                    em("vector", lambda e, ch=ch:
                       e.tensor_tensor(stgC[:, 0:1026], x2ln_ap(ch, 0, 1026), psB[:, 0:1026],
                                       ALU.mult), inc=sV)
                    em("vector", lambda e: e.tensor_scalar(stgI[:, 0:1026], stgC[:, 0:1026],
                                                           256.5, None, ALU.add), inc=sV)
                    em("vector", lambda e:
                       e.scalar_tensor_tensor(stgD[:, 0:1026], stgI[:, 0:1026], -256.5,
                                              stgC[:, 0:1026], ALU.add, ALU.is_gt), inc=sV)
                    em("vector", lambda e:
                       e.scalar_tensor_tensor(stgC[:, 0:1026], stgI[:, 0:1026], -256.0,
                                              stgD[:, 0:1026], ALU.add, ALU.subtract), inc=sV)
                    em("vector", lambda e, ch=ch:
                       e.tensor_tensor(xq_ap(ch, 0, 1026), stgC[:, 0:1026], psA[:, 0:1026],
                                       ALU.mult), inc=sV)
                v_xq = sV.n
                for b_ in (("psB", 0), ("psB", 1), ("psB", 2), ("psA", 0), ("psA", 1), ("psA", 2)):
                    consume(b_, sV)
                consume("stgCD", sV)
                if debug and s == 0 and ft == 0:
                    for ch in range(8):
                        dma("sync", dbg["xq"][0][128 * ch:128 * (ch + 1), :], xq_ap(ch, 0, 1026),
                            sO, waits=[(sV, v_xq)])

                s_sil_last = 0
                for hg in range(8):
                    wbuf = hg % 2
                    w1b = WP2[:, 4096 * wbuf: 4096 * (wbuf + 1)]
                    g_w1 = dma("sync", w1b,
                               w1q_s[:, 4096 * hg:4096 * (hg + 1)],
                               sW, waits=rwait(("w1b", wbuf), "WP2") + [(sG, g_wq_done)])
                    for hsub in range(4):
                        hc = 4 * hg + hsub
                        for bi, (b0, bn) in enumerate(W1BLOCKS):
                            for dc in range(8):
                                em("tensor", lambda e, hsub=hsub, dc=dc, b0=b0, bn=bn, w1b=w1b:
                                   e.matmul(psA[:, b0:b0 + bn],
                                            w1b[:, 512 * dc + 128 * hsub: 512 * dc + 128 * (hsub + 1)],
                                            xq_ap(dc, b0, bn), start=(dc == 0), stop=(dc == 7)),
                                   waits=([(sW, g_w1), (sV, v_xq)]
                                          + rwait(("psA", 0), ("psA", 1), ("psA", 2)))
                                   if (dc == 0 and bi == 0) else [],
                                   inc=(sT if (dc == 7 and bi == 2) else None))
                        t_h = sT.n
                        em("vector", lambda e: e.tensor_copy(stgC[:, 0:1026], psA[:, 0:1026]),
                           waits=[(sT, t_h)] + rwait("stgCD"), inc=sV)
                        for b_ in (("psA", 0), ("psA", 1), ("psA", 2)):
                            consume(b_, sV)
                        em("vector", lambda e, hc=hc:
                           e.tensor_scalar(stgD[:, 0:1024], stgC[:, 0:1024],
                                           convw[:, 3 * hc:3 * hc + 1], convb[:, hc:hc + 1],
                                           ALU.mult, ALU.add), inc=sV)
                        em("vector", lambda e, hc=hc:
                           e.scalar_tensor_tensor(stgD[:, 0:1024], stgC[:, 1:1025],
                                                  convw[:, 3 * hc + 1:3 * hc + 2], stgD[:, 0:1024],
                                                  ALU.mult, ALU.add), inc=sV)
                        em("vector", lambda e, hc=hc:
                           e.scalar_tensor_tensor(stgD[:, 0:1024], stgC[:, 2:1026],
                                                  convw[:, 3 * hc + 2:3 * hc + 3], stgD[:, 0:1024],
                                                  ALU.mult, ALU.add), inc=sV)
                        v_cv = sV.n
                        em("scalar", lambda e, hc=hc:
                           e.activation(hbuf_ap(hc, 0, 1024), stgD[:, 0:1024], AF.Silu),
                           waits=[(sV, v_cv)] + (rwait("hbufr") if hc == 0 else []), inc=sS)
                        consume_add("stgCD", sS)
                        s_sil_last = sS.n
                    consume(("w1b", wbuf), sT)

                dbg_hs = 0
                if debug and s == 0 and ft == 0:
                    for ch in range(32):
                        dbg_hs = dma("sync", dbg["hsil"][0][128 * ch:128 * (ch + 1), :],
                                     hbuf_ap(ch, 0, 1024), sO, waits=[(sS, s_sil_last)])
                for ch in range(32):
                    em("vector", lambda e, ch=ch: e.tensor_tensor(
                        stgC[:, 0:1024], hbuf_ap(ch, 0, 1024), hbuf_ap(ch, 0, 1024), ALU.mult),
                        waits=([(sS, s_sil_last)] + rwait("stgCD")
                               + ([(sO, dbg_hs)] if dbg_hs else [])) if ch == 0 else [], inc=sV)
                    if ch == 0:
                        em("vector", lambda e: e.tensor_copy(stgD[:, 0:1024], stgC[:, 0:1024]),
                           inc=sV)
                    else:
                        em("vector", lambda e: e.tensor_tensor(stgD[:, 0:1024], stgD[:, 0:1024],
                                                               stgC[:, 0:1024], ALU.max), inc=sV)
                v_tree = sV.n
                dbg_td = 0
                if debug and s == 0 and ft == 0:
                    dbg_td = dma("sync", dbg["treeD"], stgD[:, 0:1026], sO, waits=[(sV, v_tree)])
                subs_h = [(i * 128, 128) for i in range(8)]
                t_hbc = emit_qscale(8, subs_h, False, v_tree, dbg_wait=dbg_td, dbg_tap=(debug and s == 0 and ft == 0), base=40)
                em("vector", lambda e:
                   e.tensor_copy(stgC[:, 0:1024], psB[:, 0:1024]),
                   waits=[(sT, t_hbc)] + rwait("stgCD"), inc=sV)
                for ch in range(32):
                    em("vector", lambda e, ch=ch:
                       e.tensor_tensor(stgC[:, 0:1024], hbuf_ap(ch, 0, 1024), psB[:, 0:1024],
                                       ALU.mult), inc=sV)
                    em("vector", lambda e: e.tensor_scalar(stgI[:, 0:1024], stgC[:, 0:1024],
                                                           256.5, None, ALU.add), inc=sV)
                    em("vector", lambda e:
                       e.scalar_tensor_tensor(stgD[:, 0:1024], stgI[:, 0:1024], -256.5,
                                              stgC[:, 0:1024], ALU.add, ALU.is_gt), inc=sV)
                    em("vector", lambda e, ch=ch:
                       e.scalar_tensor_tensor(hbuf_ap(ch, 0, 1024), stgI[:, 0:1024], -256.0,
                                              stgD[:, 0:1024], ALU.add, ALU.subtract), inc=sV)
                v_hq = sV.n
                consume(("psB", 0), sV)
                consume(("psB", 1), sV)
                consume("stgCD", sV)
                if debug and s == 0 and ft == 0:
                    for ch in range(32):
                        dma("sync", dbg["hq"][0][128 * ch:128 * (ch + 1), :],
                            hbuf_ap(ch, 0, 1024), sO, waits=[(sV, v_hq)])

                for jc in range(8):
                    wbuf = jc % 2
                    w2b = WP1[:, 4096 * wbuf: 4096 * (wbuf + 1)]
                    g_w2 = dma("sync", w2b,
                               w2q_s[128 * jc:128 * (jc + 1), :],
                               sW, waits=rwait(("w2b", wbuf), "WP1") + [(sG, g_wq_done)])
                    pblk = 2 * (jc % 2)
                    for cb in range(2):
                        for hc in range(32):
                            em("tensor", lambda e, cb=cb, hc=hc, w2b=w2b, pblk=pblk:
                               e.matmul(psB[:, 512 * (pblk + cb): 512 * (pblk + cb + 1)],
                                        w2b[:, 128 * hc:128 * (hc + 1)],
                                        hbuf_ap(hc, 512 * cb, 512), start=(hc == 0), stop=(hc == 31)),
                               waits=([(sW, g_w2), (sV, v_hq)] + rwait(("psB", pblk + cb)))
                               if hc == 0 else [],
                               inc=(sT if hc == 31 else None))
                        t_f = sT.n
                        em("vector", lambda e, jc=jc, cb=cb, pblk=pblk:
                           e.tensor_copy(ffo_ap(jc, 512 * cb, 512),
                                         psB[:, 512 * (pblk + cb): 512 * (pblk + cb + 1)]),
                           waits=[(sT, t_f)] + (rwait("ffout") if (jc == 0 and cb == 0) else []),
                           inc=sV)
                        consume(("psB", pblk + cb), sV)
                    consume(("w2b", wbuf), sT)
                v_ff = sV.n
                consume("hbufr", sT)

                for cb in range(2):
                    c0 = 512 * cb
                    for jc in range(8):
                        em("vector", lambda e, jc=jc, c0=c0:
                           e.tensor_tensor(projs[:, 512 * jc:512 * (jc + 1)],
                                           ffo_ap(jc, c0, 512), ffo_ap(jc, c0, 512),
                                           ALU.mult),
                           waits=([(sV, v_ff)] + rwait("projs")) if jc == 0 else [], inc=sV)
                    v_sq = sV.n
                    for jc in range(8):
                        em("tensor", lambda e, jc=jc, c0=c0:
                           e.matmul(psC[0:1, 0:512], onescol[:], ffo_ap(jc, c0, 512),
                                    start=(jc == 0), stop=(jc == 7)),
                           waits=([(sV, v_ff)] + rwait("psC")) if jc == 0 else [],
                           inc=(sT if jc == 7 else None))
                    t_sum = sT.n
                    for jc in range(8):
                        em("tensor", lambda e, jc=jc:
                           e.matmul(psC[32:33, 0:512], onescol[:], projs[:, 512 * jc:512 * (jc + 1)],
                                    start=(jc == 0), stop=(jc == 7)),
                           waits=[(sV, v_sq)] if jc == 0 else [],
                           inc=(sT if jc == 7 else None))
                    t_ssq = sT.n
                    consume_add("projs", sT)
                    s_rstd, v_Brow = emit_ln_rows(t_sum, t_ssq)
                    t_bc = emit_AB_bcast(s_rstd, v_Brow)
                    for jc in range(8):
                        em("vector", lambda e, jc=jc, c0=c0:
                           e.tensor_tensor(stgA[:, 0:512], ffo_ap(jc, c0, 512),
                                           psA[:, 0:512], ALU.mult),
                           waits=[(sT, t_bc)] if jc == 0 else [], inc=sV)
                        em("vector", lambda e:
                           e.tensor_tensor(stgB[:, 0:512], stgA[:, 0:512], psA[:, 512:1024],
                                           ALU.add), inc=sV)
                        em("vector", lambda e, jc=jc, c0=c0:
                           e.tensor_tensor(outstg[:, 0:512], stgB[:, 0:512],
                                           x1b_ap(jc, c0 + 1, 512), ALU.add),
                           waits=rwait("outstg"), inc=sV)
                        v_o = sV.n
                        dma("sync", outd[s][128 * jc:128 * (jc + 1), t0 + c0: t0 + c0 + 512],
                            outstg[:, 0:512], sO, waits=[(sV, v_o)])
                        consume("outstg", sO)
                    consume(("psA", 0), sV)
                    consume(("psA", 1), sV)
                    consume_add("projs", sV)
                consume("ffout", sV)
                consume("x1b", sV)
                consume_add("x1b", sT)

        em("sync", lambda e: None, waits=[(sO, sO.n)])

        block = E(nc.Block())

        @block.sync
        def _(e):
            for f in Q["sync"]:
                f(e)

        @block.gpsimd
        def _(e):
            for f in Q["gpsimd"]:
                f(e)

        @block.tensor
        def _(e):
            for f in Q["tensor"]:
                f(e)

        @block.vector
        def _(e):
            for f in Q["vector"]:
                f(e)

        @block.scalar
        def _(e):
            for f in Q["scalar"]:
                f(e)

    return nc, dbg


_NC_CACHE = {}


def _host_consts():
    inv_freq = 1.0 / 10000.0 ** (np.arange(0, DR, 2, dtype=np.float32) / DR)
    t = np.arange(T, dtype=np.float32)
    freqs = np.outer(inv_freq, t)
    cosT = np.tile(np.cos(freqs), (8, 1)).astype(np.float32)
    sinT = np.tile(np.sin(freqs), (8, 1)).astype(np.float32)
    ident = np.eye(128, dtype=np.float32)
    onesr = np.ones((1, 128), np.float32)
    e2 = np.zeros((2, 128), np.float32)
    e2[0, 0:64] = 1.0
    e2[1, 64:128] = 1.0
    j = np.arange(128)
    atri = (j[:, None] < j[None, :]).astype(np.float32)
    btri = (-300.0 * (j[:, None] >= j[None, :])).astype(np.float32)
    ocole = np.ones((128, 1), np.float32)
    ocole[64] = 0.0
    ocole[97:] = 0.0
    ocolo = np.ones((128, 1), np.float32)
    ocolo[32:64] = 0.0
    d = dict(cosTc=cosT, sinTc=sinT, identc=ident, onesr=onesr, e2c=e2,
             atri=atri, btri=btri, ocole=ocole, ocolo=ocolo)
    out = {k: np.ascontiguousarray(v.astype(ml_dtypes.bfloat16)) for k, v in d.items()}
    out["identf"] = np.eye(128, dtype=np.float32)
    return out


def _rope_perm():
    perm = []
    for ab in range(2):
        for half in range(2):
            for hh in range(8):
                h = 8 * half + hh
                for jj in range(16):
                    perm.append(32 * h + 16 * ab + jj)
    return np.array(perm, dtype=np.int64)


def kernel(L0_attn_x, L0_attn_W_kv, L0_attn_g_kv, L0_attn_W_kdec, L0_attn_W_vdec,
           L0_attn_W_q, L0_attn_W_qdec, L0_attn_W_krope, L0_attn_W_qrope, L0_attn_W_o,
           L0_ffn_w1, L0_ffn_conv_w, L0_ffn_conv_b, L0_ffn_w2, _debug=0):
    x = np.ascontiguousarray(np.asarray(L0_attn_x, dtype=np.float32))
    ncores = 8
    perm = _rope_perm()
    consts = _host_consts()
    common = dict(
        wkv=np.ascontiguousarray(np.asarray(L0_attn_W_kv, np.float32)),
        gkv=np.ascontiguousarray(np.asarray(L0_attn_g_kv, np.float32).reshape(KVR, 1)),
        wq=np.ascontiguousarray(np.asarray(L0_attn_W_q, np.float32)),
        wkrope=np.ascontiguousarray(np.asarray(L0_attn_W_krope, np.float32)[:, perm]),
        wqrope=np.ascontiguousarray(np.asarray(L0_attn_W_qrope, np.float32)[:, perm]),
        wkdec=np.ascontiguousarray(np.asarray(L0_attn_W_kdec, np.float32)),
        wvdec=np.ascontiguousarray(np.asarray(L0_attn_W_vdec, np.float32)),
        wqdec=np.ascontiguousarray(np.asarray(L0_attn_W_qdec, np.float32)),
        wo=np.ascontiguousarray(np.asarray(L0_attn_W_o, np.float32)),
        w1t=np.ascontiguousarray(np.asarray(L0_ffn_w1, np.float32).T.reshape(
            8, 128, 8, 512).transpose(1, 2, 0, 3).reshape(128, 32768)),
        w2t=np.ascontiguousarray(np.asarray(L0_ffn_w2, np.float32).T.reshape(
            32, 128, 8, 128).transpose(2, 1, 0, 3).reshape(1024, 4096)),
        convw=np.ascontiguousarray(np.asarray(L0_ffn_conv_w, np.float32).reshape(FH, 3)),
        convb=np.ascontiguousarray(np.asarray(L0_ffn_conv_b, np.float32).reshape(FH, 1)),
        **consts,
    )
    key = ("prog", _debug)
    if key not in _NC_CACHE:
        _NC_CACHE[key] = build_program(debug=_debug)
    nc, dbg = _NC_CACHE[key]
    in_maps = []
    for c in range(ncores):
        m = dict(common)
        m["xin"] = np.ascontiguousarray(x[2 * c:2 * c + 2].transpose(0, 2, 1))
        in_maps.append(m)
    res = run_bass_kernel_spmd(nc, in_maps, core_ids=list(range(ncores)))
    outs = []
    for c in range(ncores):
        o = np.asarray(res.results[c]["outd"])
        outs.append(o.transpose(0, 2, 1))
    full = np.concatenate(outs, axis=0).astype(np.float32)
    kernel._last_results = res
    return full
